# version 25
# baseline (speedup 1.0000x reference)
import numpy as np
import ml_dtypes
DBG_RES = None
import concourse.bass as bass
import concourse.mybir as mybir
import concourse.bass_utils as _bu
from concourse.bass_utils import run_bass_kernel_spmd
from contextlib import ExitStack

F32 = mybir.dt.float32
BF16 = mybir.dt.bfloat16
I32 = mybir.dt.int32
AF = mybir.ActivationFunctionType
ALU = mybir.AluOpType
AX = mybir.AxisListType
BF = ml_dtypes.bfloat16

_orig_bvo = _bu.bir_verify_and_optimise
def _bvo_fast(tmpdir, inp="bir.json", outp="file.neff", arch=None, *, dve_root=None):
    real_run = _bu.run_command
    def run_patched(cmd, cwd=None, **kw):
        cmd = ["--enable-birsim=false" if c == "--enable-birsim=true" else c for c in cmd]
        return real_run(cmd, cwd=cwd, **kw)
    _bu.run_command = run_patched
    try:
        return _orig_bvo(tmpdir, inp, outp, arch, dve_root=dve_root)
    finally:
        _bu.run_command = real_run
_bu.bir_verify_and_optimise = _bvo_fast

D = 1024
H = 16
DH = 64
TOK = 4096
NCH = 32
F = 4096
EPS_ATTN = 1e-8
EPS_LN = 1e-5

ENG_SEM = {"tensor": "pe", "vector": "dv", "scalar": "ac", "gpsimd": "po"}
DMA_SEM = {"sync": "sd", "gpsimd": "gd"}
DTSIZE = {F32: 4, BF16: 2, I32: 4}


class Sched:

    NLANES = 8

    def __init__(self):
        self.ops = {k: [] for k in ("sync", "tensor", "vector", "scalar", "gpsimd")}
        self.counts = {}
        for s in ("pe", "dv", "ac", "po"):
            self.counts[s] = 0
        for base in ("sd", "gd", "sr"):
            for l in range(self.NLANES):
                self.counts[base + str(l)] = 0
        self.lane = {"sd": 0, "gd": 0, "sr": 0}
        self.bufs = {}
        self.floor = {}

    def _mark(self, name):
        if name not in self.bufs:
            self.bufs[name] = {"w": None, "r": []}
        return self.bufs[name]

    def op(self, engine, build, reads=(), writes=(), dma=False, sem=None, label=""):
        if dma:
            base = sem or DMA_SEM[engine]
            sem = base + str(self.lane[base])
            self.lane[base] = (self.lane[base] + 1) % self.NLANES
            inc = 16
        else:
            sem = ENG_SEM[engine]
            inc = 1
        waits = {}
        def add_wait(mark):
            if mark is None:
                return
            s, v = mark
            if not dma and s == sem and engine == "tensor":
                return
            if waits.get(s, 0) < v:
                waits[s] = v
        for b in reads:
            add_wait(self._mark(b)["w"])
        for b in writes:
            m = self._mark(b)
            add_wait(m["w"])
            for rm in m["r"]:
                add_wait(rm)
        fl = []
        for s, v in waits.items():
            key = (engine, s)
            if self.floor.get(key, 0) < v:
                fl.append((s, v))
                self.floor[key] = v
        self.counts[sem] += inc
        mark = (sem, self.counts[sem])
        for b in reads:
            self._mark(b)["r"].append(mark)
        for b in writes:
            m = self._mark(b)
            m["w"] = mark
            m["r"] = []
        self.ops[engine].append((tuple(fl), build, sem, inc, label))
        return mark

    def wait_all(self, engine, marks):
        fl = []
        for s, v in marks:
            if s == ENG_SEM.get(engine):
                continue
            key = (engine, s)
            if self.floor.get(key, 0) < v:
                fl.append((s, v))
                self.floor[key] = v
        if fl:
            self.ops[engine].append((tuple(fl), None, None, 0, "barrier"))

    def snapshot(self):
        return [(s, v) for s, v in self.counts.items() if v > 0]

    def replay(self, engine, eng, sems):
        for fl, build, sem, inc, label in self.ops[engine]:
            for s, v in fl:
                eng.wait_ge(sems[s], v)
            if build is not None:
                build(eng).then_inc(sems[sem], inc)


STAGES = [NCH, NCH, 8]
DEBUG = False

def build_program():
    n_pref, n_own, n_b = STAGES
    nc = bass.Bass("TRN2", target_bir_lowering=False)

    xo_h = nc.dram_tensor("xo", [TOK, D], F32, kind="ExternalInput")
    xp_h = nc.dram_tensor("xp", [TOK, D], F32, kind="ExternalInput")
    amask_h = nc.dram_tensor("amask", [1, 1], F32, kind="ExternalInput")
    wqkv_h = nc.dram_tensor("wqkvT", [D, 3 * D], F32, kind="ExternalInput")
    wo_h = nc.dram_tensor("woT", [D, D], F32, kind="ExternalInput")
    fc1q_h = nc.dram_tensor("fc1q", [D, F], I32, kind="ExternalInput")
    fc1s_h = nc.dram_tensor("fc1s", [8, F], F32, kind="ExternalInput")
    fc1sb_h = nc.dram_tensor("fc1sb", [8, 128, F], F32, kind="ExternalInput")
    fc2sb_h = nc.dram_tensor("fc2sb", [32, 128, D], F32, kind="ExternalInput")
    fc1b_h = nc.dram_tensor("fc1b", [128, 32], F32, kind="ExternalInput")
    fc2q_h = nc.dram_tensor("fc2q", [F, D], I32, kind="ExternalInput")
    fc2s_h = nc.dram_tensor("fc2s", [32, D], F32, kind="ExternalInput")
    fc2b_h = nc.dram_tensor("fc2b", [1, D], F32, kind="ExternalInput")
    gam_h = nc.dram_tensor("gam", [1, D], F32, kind="ExternalInput")
    bet_h = nc.dram_tensor("bet", [1, D], F32, kind="ExternalInput")
    gamc_h = nc.dram_tensor("gamC", [128, 8], F32, kind="ExternalInput")
    betc_h = nc.dram_tensor("betC", [128, 8], F32, kind="ExternalInput")
    umask_h = nc.dram_tensor("umaskI", [128, 128], F32, kind="ExternalInput")
    ident_h = nc.dram_tensor("identI", [128, 128], F32, kind="ExternalInput")
    out_h = nc.dram_tensor("out", [TOK, D], F32, kind="ExternalOutput")
    dbg = "ExternalOutput" if DEBUG else "Internal"
    x1_d = nc.dram_tensor("x1d", [TOK, D], BF16, kind=dbg)
    w1_d = nc.dram_tensor("w1d", [256 * 128, 128], BF16, kind=dbg)
    w2_d = nc.dram_tensor("w2d", [F, D], BF16, kind=dbg)

    S = Sched()
    op = S.op

    with ExitStack() as ctx:
        def pers(name, shape, dtype):
            return ctx.enter_context(nc.sbuf_tensor(name, shape, dtype))

        state_f = pers("state_f", [128, 512], F32)
        state_b = pers("state_b", [128, 1024], BF16)
        ksr_f = pers("ksr_f", [1, D], F32)
        ksr_b = pers("ksr_b", [128, D], BF16)
        umask = pers("umask", [128, 128], BF16)
        ident = pers("ident", [128, 128], BF16)
        ones_fb = pers("ones_fb", [128, 128], BF16)
        ones_ff = pers("ones_ff", [128, 128], F32)
        ones_c = pers("ones_c", [128, 1], BF16)
        amask_s = pers("amask_s", [1, 1], F32)
        b1_s = pers("b1_s", [128, 32], F32)
        gamC = pers("gamC_s", [128, 8], F32)
        betC = pers("betC_s", [128, 8], F32)
        stx = pers("stx", [128, 4], F32)
        bty = pers("bty", [128, 16], F32)
        bt1 = bty.ap()[:, 0:4]
        bt2 = bty.ap()[:, 4:8]
        bt4 = bty.ap()[:, 8:12]
        bt5 = bty.ap()[:, 12:16]
        btx = pers("btx", [128, 16], F32)

        arena_base = (nc.sbuf_base + 63) & ~63
        arena_top = nc.sbuf_top

        def make_arena(base):
            cur = [base]
            def alloc(name, shape, dtype):
                nbytes = shape[1] * DTSIZE[dtype]
                off = cur[0]
                cur[0] = (off + nbytes + 63) & ~63
                assert cur[0] <= arena_top, f"sbuf arena overflow at {name}: {cur[0]} > {arena_top}"
                return nc.alloc_sbuf_tensor_at(name, shape, dtype, offset=off)
            return alloc, cur

        A, acur = make_arena(arena_base)
        wqkv = A("wqkv", [128, 8 * 3 * D], BF16)
        wo_s = A("wo_s", [128, 8 * D], BF16)
        xa = A("xa", [128, D], BF16)
        xb = A("xb", [128, D], BF16)
        x_fma = A("x_fma", [128, D], BF16)
        x_fmb = A("x_fmb", [128, D], BF16)
        t1 = A("t1", [128, D], BF16)
        t2 = A("t2", [128, D], BF16)
        qhat = A("qhat", [128, D], BF16)
        khat = A("khat", [128, D], BF16)
        v_tm = A("v_tm", [128, D], BF16)
        k_fm = A("k_fm", [128, 2 * D], BF16)
        qp_tm = A("qp_tm", [128, D], BF16)
        qp_fm = A("qp_fm", [128, D], BF16)
        st_m = A("st_m", [128, 1024], BF16)
        attn = A("attn", [128, D], BF16)
        r_sb = A("r_sb", [128, D], F32)
        dr32 = A("dr32", [128, 32], F32)
        g1base = acur[0]
        G, gcur = make_arena(g1base)
        g1a = G("g1a", [128, F], F32)
        g1b = G("g1b", [128, F], F32)
        g2a = G("g2a", [128, D], F32)
        g2b = G("g2b", [128, D], F32)
        wsl_a = G("wsl_a", [128, 512], BF16)
        wsl_b = G("wsl_b", [128, 512], BF16)
        wsl_c = G("wsl_c", [128, 512], BF16)
        wsl_d = G("wsl_d", [128, 512], BF16)
        s1b_a = G("s1b_a", [128, F], F32)
        s1b_b = G("s1b_b", [128, F], F32)
        s2b_a = G("s2b_a", [128, D], F32)
        s2b_b = G("s2b_b", [128, D], F32)
        B, bcur = make_arena(g1base)
        w1r_a = B("w1r_a", [128, 4096], BF16)
        w1r_b = B("w1r_b", [128, 4096], BF16)
        w2r_a = B("w2r_a", [128, 4096], BF16)
        w2r_b = B("w2r_b", [128, 4096], BF16)
        y_fm = B("y_fm", [128, 8 * 512], BF16)
        x1_s = B("x1_s", [128, 4 * D], BF16)
        x1_s2 = B("x1_s2", [128, 4 * D], BF16)
        h_fm = B("h_fm", [128, 4 * D], BF16)
        u_bf_a = B("u_bf_a", [128, D], BF16)
        u_bf_b = B("u_bf_b", [128, D], BF16)
        b2h_b = B("b2h_b", [128, D], BF16)
        u_s = B("u_s", [128, D], F32)
        r2_s = B("r2_s", [128, D], F32)
        z_sb = B("z_sb", [128, 4 * D], F32)
        out_s = B("out_s", [128, D], F32)

        PS0 = ctx.enter_context(nc.psum_tensor("PS0", [128, 2048], F32))
        PS1 = ctx.enter_context(nc.psum_tensor("PS1", [128, 2048], F32))
        PS = [PS0, PS1]
        def bank(g, k):
            return f"p{g}{'abcd'[k]}"
        def banksr(g, lo, hi):
            return [bank(g, k) for k in range(lo // 512, (hi + 511) // 512)]

        semnames = ["pe", "dv", "ac", "po"] + [b + str(l) for b in ("sd", "gd", "sr")
                                               for l in range(Sched.NLANES)]
        sems = {n: ctx.enter_context(nc.semaphore(n)) for n in semnames}

        op("vector", lambda e: e.memset(state_f[:], 0.0), writes=["state_f"])
        op("vector", lambda e: e.memset(state_b[:], 0.0), writes=["state_b"])
        op("vector", lambda e: e.memset(ksr_f[:], 0.0), writes=["ksr_f"])
        op("vector", lambda e: e.memset(ksr_b[:], 0.0), writes=["ksr_b"])
        op("vector", lambda e: e.memset(ones_fb[:], 1.0), writes=["ones_fb"])
        op("vector", lambda e: e.memset(ones_ff[:], 1.0), writes=["ones_ff"])
        op("vector", lambda e: e.memset(ones_c[:], 1.0), writes=["ones_c"])
        op("vector", lambda e: e.memset(k_fm[:], 0.0), writes=["k_fm"])
        op("sync", lambda e: e.dma_start(out=amask_s[:], in_=amask_h.ap()), dma=True, sem="sr", writes=["amask_s"])
        op("sync", lambda e: e.dma_start(out=b1_s[:], in_=fc1b_h.ap()), dma=True, writes=["b1_s"])
        op("sync", lambda e: e.dma_start(out=gamC[:], in_=gamc_h.ap()), dma=True, writes=["gamC"])
        op("sync", lambda e: e.dma_start(out=betC[:], in_=betc_h.ap()), dma=True, writes=["betC"])
        op("gpsimd", lambda e: e.dma_start(out=umask[:], in_=umask_h.ap()), dma=True, writes=["umask"])
        op("gpsimd", lambda e: e.dma_start(out=ident[:], in_=ident_h.ap()), dma=True, writes=["ident"])
        wq_in = wqkv_h.ap().rearrange("(c p) n -> p c n", p=128)
        for dd in range(8):
            op("gpsimd", lambda e, dd=dd: e.dma_start(out=wqkv[:, 3072 * dd:3072 * dd + 3072],
                                                      in_=wq_in[:, dd, :]),
               dma=True, writes=[f"wqkv{dd}"])
        wo_in = wo_h.ap().rearrange("(c p) n -> p c n", p=128)
        for dd in range(8):
            op("gpsimd", lambda e, dd=dd: e.dma_start(out=wo_s[:, 1024 * dd:1024 * dd + 1024],
                                                      in_=wo_in[:, dd, :]),
               dma=True, writes=[f"wo{dd}"])

        def x_load_transpose(src_h, c, xbuf, xname, tg):
            op("gpsimd", lambda e: e.dma_start(out=xbuf[:], in_=src_h.ap()[128 * c:128 * c + 128, :]),
               dma=True, writes=[xname])
            for d in range(8):
                op("tensor", lambda e, d=d: e.matmul(PS[tg][:, 1024 + 128 * d:1152 + 128 * d],
                                                     xbuf[:, 128 * d:128 * d + 128], ident[:],
                                                     start=True, stop=True),
                   reads=[xname, "ident"], writes=banksr(tg, 1024 + 128 * d, 1152 + 128 * d))
            op("scalar", lambda e: e.copy(out=x_fma[:], in_=PS[tg][:, 1024:2048]),
               reads=[bank(tg, 2), bank(tg, 3)], writes=["x_fma"])

        def x_load_pref(src_h, c):
            xbuf, xname = (xa, "xa") if c % 2 == 0 else (xb, "xb")
            op("gpsimd", lambda e, xbuf=xbuf, c=c: e.dma_start(
                out=xbuf[:], in_=src_h.ap()[128 * c:128 * c + 128, :]),
               dma=True, writes=[xname])

        def x_transpose_pipe(c):
            j = c % 2
            xbuf, xname = (xa, "xa") if c % 2 == 0 else (xb, "xb")
            xfm, xfmn = (x_fma, "x_fma") if j == 0 else (x_fmb, "x_fmb")
            for w in range(2):
                for d in range(4):
                    dd = 4 * w + d
                    op("tensor", lambda e, dd=dd, d=d, xbuf=xbuf: e.matmul(
                        PS[j][:, 1536 + 128 * d:1664 + 128 * d],
                        xbuf[:, 128 * dd:128 * dd + 128], ident[:],
                        start=True, stop=True),
                       reads=[xname, "ident"], writes=[bank(j, 3)])
                op("vector", lambda e, w=w, xfm=xfm: e.tensor_copy(
                    out=xfm[:, 512 * w:512 * w + 512], in_=PS[j][:, 1536:2048]),
                   reads=[bank(j, 3)], writes=[xfmn])

        def kv_mms(g, xfm, xfmn):
            for d in range(8):
                for s in range(4):
                    op("tensor", lambda e, d=d, s=s, xfm=xfm: e.matmul(
                        PS[g][:, 512 * s:512 * s + 512],
                        xfm[:, 128 * d:128 * d + 128],
                        wqkv[:, 3072 * d + D + 512 * s: 3072 * d + D + 512 * s + 512],
                        start=(d == 0), stop=(d == 7)),
                       reads=[xfmn, f"wqkv{d}"], writes=[bank(g, s)])

        def elu_acts(g):
            op("scalar", lambda e: e.activation(out=t1[:], in_=PS[g][:, 0:1024], func=AF.Relu, scale=-1.0),
               reads=[bank(g, 0), bank(g, 1)], writes=["t1"])
            op("scalar", lambda e: e.activation(out=t2[:], in_=t1[:], func=AF.Exp, scale=-1.0),
               reads=["t1"], writes=["t2"])
            op("scalar", lambda e: e.activation(out=t1[:], in_=PS[g][:, 0:1024], func=AF.Relu),
               reads=[bank(g, 0), bank(g, 1)], writes=["t1"])

        def khat_add():
            op("vector", lambda e: e.tensor_tensor(out=khat[:], in0=t2[:], in1=t1[:], op=ALU.add),
               reads=["t2", "t1"], writes=["khat"])

        def v_evac(g):
            op("vector", lambda e: e.tensor_copy(out=v_tm[:], in_=PS[g][:, 1024:2048]),
               reads=[bank(g, 2), bank(g, 3)], writes=["v_tm"])

        def elu_v(g):
            elu_acts(g)
            khat_add()
            v_evac(g)

        def state_update(g, prefix):
            for h in range(16):
                op("tensor", lambda e, h=h: e.matmul(
                    PS[g][64 * (h % 2):64 * (h % 2) + 64, 64 * (h // 2):64 * (h // 2) + 64],
                    khat[:, 64 * h:64 * h + 64], v_tm[:, 64 * h:64 * h + 64],
                    start=True, stop=True),
                   reads=["khat", "v_tm"], writes=[bank(g, 0)])
            op("vector", lambda e: e.tensor_tensor(out=state_f[:], in0=PS[g][:, 0:512], in1=state_f[:], op=ALU.add),
               reads=[bank(g, 0), "state_f"], writes=["state_f"])
            if not prefix:
                op("vector", lambda e: e.tensor_copy(
                    out=state_b.ap()[0:64, :].rearrange("p (c n) -> p c n", n=64)[:, 0:16:2, :],
                    in_=state_f.ap()[0:64, :].rearrange("p (c n) -> p c n", n=64)),
                   reads=["state_f"], writes=["state_b"])
                op("vector", lambda e: e.tensor_copy(
                    out=state_b.ap()[64:128, :].rearrange("p (c n) -> p c n", n=64)[:, 1:16:2, :],
                    in_=state_f.ap()[64:128, :].rearrange("p (c n) -> p c n", n=64)),
                   reads=["state_f"], writes=["state_b"])
            for s in range(2):
                bk = bank(g, 1 + s)
                lo = 512 + 512 * s
                op("tensor", lambda e, s=s, lo=lo: e.matmul(
                    PS[g][0:1, lo:lo + 512], ones_c[:], khat[:, 512 * s:512 * s + 512],
                    start=True, stop=True),
                   reads=["ones_c", "khat"], writes=[bk])
                if prefix:
                    op("vector", lambda e, s=s, lo=lo: e.scalar_tensor_tensor(
                        out=ksr_f[0:1, 512 * s:512 * s + 512], in0=PS[g][0:1, lo:lo + 512],
                        scalar=amask_s[0:1, :], in1=ksr_f[0:1, 512 * s:512 * s + 512],
                        op0=ALU.mult, op1=ALU.add),
                       reads=[bk, "ksr_f", "amask_s"], writes=["ksr_f"])
                else:
                    op("vector", lambda e, s=s, lo=lo: e.tensor_tensor(
                        out=ksr_f[0:1, 512 * s:512 * s + 512], in0=PS[g][0:1, lo:lo + 512],
                        in1=ksr_f[0:1, 512 * s:512 * s + 512], op=ALU.add),
                       reads=[bk, "ksr_f"], writes=["ksr_f"])
            if not prefix:
                op("vector", lambda e: e.tensor_copy(out=ksr_b[0:1, :], in_=ksr_f[:]),
                   reads=["ksr_f"], writes=["ksr_b"])

        def dequant_slice(qgran, gname, scltile, sclname, outap, wd_name, wsl, wslname, scol):
            op("vector", lambda e: e.scalar_tensor_tensor(
                out=wsl[:], in0=qgran[:, scol:scol + 512], scalar=-128.0,
                in1=scltile[:, scol:scol + 512], op0=ALU.add, op1=ALU.mult),
               reads=[gname, sclname], writes=[wslname])
            op("sync", lambda e: e.dma_start(out=outap, in_=wsl[:]),
               dma=True, reads=[wslname], writes=[wd_name])

        def layer_norm_tail(src, src_reads, outap, outname, s1, s2, s4, s5, n1, n2, n4, n5,
                            acc_ap=None, acc_w=[]):
            op("vector", lambda e: e.tensor_scalar_mul(s1, s1, 1.0 / D),
               reads=[n1], writes=[n1])
            op("vector", lambda e: e.tensor_tensor(out=s5, in0=s1, in1=s1, op=ALU.mult),
               reads=[n1], writes=[n5])
            op("vector", lambda e: e.scalar_tensor_tensor(
                out=s2, in0=s2, scalar=1.0 / D, in1=s5, op0=ALU.mult, op1=ALU.subtract),
               reads=[n2, n5], writes=[n2])
            op("vector", lambda e: e.tensor_scalar_add(s2, s2, EPS_LN),
               reads=[n2], writes=[n2])
            op("scalar", lambda e: e.activation(out=s4, in_=s2, func=AF.Ln),
               reads=[n2], writes=[n4])
            op("scalar", lambda e: e.activation(out=s4, in_=s4, func=AF.Exp, scale=-0.5),
               reads=[n4], writes=[n4])
            op("vector", lambda e: e.scalar_tensor_tensor(
                out=s5, in0=s1, scalar=-1.0, in1=s4, op0=ALU.mult, op1=ALU.mult),
               reads=[n1, n4], writes=[n5])
            op("scalar", lambda e: e.activation(out=outap, in_=src, func=AF.Identity,
                                                bias=s5, scale=s4, accum_out=acc_ap),
               reads=list(src_reads) + [n4, n5], writes=[outname] + acc_w)

        B_QUEUE = []

        def bq(fn):
            B_QUEUE.append(fn)

        def ln_rstd(b1, b2, b4, b5, n1, n2, n4, n5):
            op("vector", lambda e: e.tensor_scalar_mul(b1, b1, 1.0 / D),
               reads=[n1], writes=[n1])
            op("vector", lambda e: e.tensor_tensor(out=b5, in0=b1, in1=b1, op=ALU.mult),
               reads=[n1], writes=[n5])
            op("vector", lambda e: e.scalar_tensor_tensor(
                out=b2, in0=b2, scalar=1.0 / D, in1=b5, op0=ALU.mult, op1=ALU.subtract),
               reads=[n2, n5], writes=[n2])
            op("vector", lambda e: e.tensor_scalar_add(b2, b2, EPS_LN),
               reads=[n2], writes=[n2])
            op("scalar", lambda e: e.activation(out=b4, in_=b2, func=AF.Ln),
               reads=[n2], writes=[n4])
            op("scalar", lambda e: e.activation(out=b4, in_=b4, func=AF.Exp, scale=-0.5),
               reads=[n4], writes=[n4])
            op("vector", lambda e: e.scalar_tensor_tensor(
                out=b5, in0=b1, scalar=-1.0, in1=b4, op0=ALU.mult, op1=ALU.mult),
               reads=[n1, n4], writes=[n5])

        PENDING = [None]

        def emit_b_super(sc):
            x1p, x1pn = (x1_s, "x1_s") if sc % 2 == 0 else (x1_s2, "x1_s2")
            ln_rstd(bt1, bt2, bt4, bt5, "bty", "bty", "bty", "bty")
            def apply_prep(t):
                ub, ubn = (u_bf_a, "u_bf_a") if t % 2 == 0 else (u_bf_b, "u_bf_b")
                def f(t=t, ub=ub, ubn=ubn):
                    op("scalar", lambda e, t=t, ub=ub: e.activation(
                        out=ub[:], in_=x1p[:, D * t:D * t + D], func=AF.Identity,
                        bias=bt5[:, t:t + 1], scale=bt4[:, t:t + 1]),
                       reads=[x1pn, "bty"], writes=[ubn])
                return f

            def apply_pe(t):
                ub, ubn = (u_bf_a, "u_bf_a") if t % 2 == 0 else (u_bf_b, "u_bf_b")
                def f(t=t, ub=ub, ubn=ubn):
                    for d in range(8):
                        op("tensor", lambda e, d=d, ub=ub: e.matmul(
                            PS1[:, 1024 + 128 * d:1152 + 128 * d],
                            ub[:, 128 * d:128 * d + 128], ident[:],
                            start=True, stop=True),
                           reads=[ubn, "ident"], writes=banksr(1, 1024 + 128 * d, 1152 + 128 * d))
                    for d in range(8):
                        op("scalar", lambda e, d=d, t=t: e.activation(
                            out=h_fm[:, 512 * d + 128 * t:512 * d + 128 * t + 128],
                            in_=PS1[:, 1024 + 128 * d:1152 + 128 * d], func=AF.Identity,
                            bias=betC[:, d:d + 1], scale=gamC[:, d:d + 1]),
                           reads=[bank(1, 2 + d // 4), "gamC", "betC"], writes=["h_fm"])
                return f
            bq(apply_prep(0))
            bq(apply_prep(1))
            bq(apply_pe(0))
            bq(apply_prep(2))
            bq(apply_pe(1))
            bq(apply_prep(3))
            bq(apply_pe(2))
            bq(apply_pe(3))
            def w1_load(mg):
                w1r, w1rn = (w1r_a, "w1r_a") if mg % 2 == 0 else (w1r_b, "w1r_b")
                def f(mg=mg, w1r=w1r, w1rn=w1rn):
                    op("sync", lambda e: e.dma_start(
                        out=w1r.ap().rearrange("p (t n) -> p t n", t=32),
                        in_=w1_d.ap().rearrange("(t p) n -> p t n", p=128)[:, 32 * mg:32 * mg + 32, :]),
                       dma=True, reads=["w1d"], writes=[w1rn])
                return f
            bq(w1_load(0))
            bq(w1_load(1))
            for q in range(4):
                def w2_loads(q=q):
                    for j, (buf, bn) in enumerate(((w2r_a, "w2r_a"), (w2r_b, "w2r_b"))):
                        e8 = 2 * q + j
                        op("sync", lambda e, buf=buf, e8=e8: e.dma_start(
                            out=buf.ap().rearrange("p (m n) -> p m n", m=4),
                            in_=w2_d.ap().rearrange("(m p) n -> p m n", p=128)[:, 4 * e8:4 * e8 + 4, :]),
                           dma=True, reads=["w2d"], writes=[bn])
                bq(w2_loads)
                for gi in range(2):
                    mg = 2 * q + gi
                    w1r, w1rn = (w1r_a, "w1r_a") if mg % 2 == 0 else (w1r_b, "w1r_b")
                    def fc1_item(mg=mg, w1r=w1r, w1rn=w1rn):
                        for mm in range(4):
                            m = 4 * mg + mm
                            yb = bank(0, m % 2)
                            lo = 512 * (m % 2)
                            for d in range(8):
                                op("tensor", lambda e, mm=mm, d=d, lo=lo, w1r=w1r: e.matmul(
                                    PS0[:, lo:lo + 512],
                                    w1r[:, 128 * (8 * mm + d):128 * (8 * mm + d) + 128],
                                    h_fm[:, 512 * d:512 * d + 512],
                                    start=(d == 0), stop=(d == 7)),
                                   reads=[w1rn, "h_fm"], writes=[yb])
                            op("scalar", lambda e, m=m, lo=lo: e.activation(
                                out=y_fm[:, 512 * (m % 8):512 * (m % 8) + 512], in_=PS0[:, lo:lo + 512],
                                func=AF.Silu, bias=b1_s[:, m:m + 1]),
                               reads=[yb, "b1_s"], writes=["y_fm"])
                    bq(fc1_item)
                    if mg + 2 < 8:
                        bq(w1_load(mg + 2))
                for t in range(4):
                    def fc2_item(q=q, t=t, sc=sc):
                        for m in range(8 * q, 8 * q + 8):
                            rbuf, rname = (w2r_a, "w2r_a") if (m // 4) % 2 == 0 else (w2r_b, "w2r_b")
                            for s in range(2):
                                op("tensor", lambda e, m=m, s=s, rbuf=rbuf: e.matmul(
                                    PS0[:, 1024 + 512 * s:1536 + 512 * s],
                                    y_fm[:, 512 * (m % 8) + 128 * t: 512 * (m % 8) + 128 * t + 128],
                                    rbuf[:, 1024 * (m % 4) + 512 * s: 1024 * (m % 4) + 512 * s + 512],
                                    start=(m % 8 == 0), stop=(m % 8 == 7)),
                                   reads=["y_fm", rname], writes=[bank(0, 2 + s)])
                        if q == 0:
                            op("vector", lambda e, t=t: e.tensor_copy(
                                out=z_sb[:, D * t:D * t + D], in_=PS0[:, 1024:2048]),
                               reads=["p0c", "p0d"], writes=["z_sb"])
                        else:
                            op("vector", lambda e, t=t: e.tensor_tensor(
                                out=z_sb[:, D * t:D * t + D], in0=PS0[:, 1024:2048],
                                in1=z_sb[:, D * t:D * t + D], op=ALU.add),
                               reads=["p0c", "p0d", "z_sb"], writes=["z_sb"])
                        if q == 3:
                            op("vector", lambda e, t=t: e.scalar_tensor_tensor(
                                out=u_s[:], in0=z_sb[:, D * t:D * t + D], scalar=0.5, in1=b2h_b[:],
                                op0=ALU.mult, op1=ALU.add),
                               reads=["z_sb", "b2h_b"], writes=["u_s"])
                            op("vector", lambda e, t=t: e.scalar_tensor_tensor(
                                out=r2_s[:], in0=u_s[:], scalar=1.0, in1=x1p[:, D * t:D * t + D],
                                op0=ALU.mult, op1=ALU.add, accum_out=btx[:, 8 + t:9 + t]),
                               reads=["u_s", x1pn], writes=["r2_s", "btx"])
                            op("vector", lambda e: e.scalar_tensor_tensor(
                                out=u_s[:], in0=r2_s[:], scalar=1.0, in1=r2_s[:],
                                op0=ALU.mult, op1=ALU.mult, accum_out=btx[:, 12 + t:13 + t]),
                               reads=["r2_s"], writes=["u_s", "btx"])
                    bq(fc2_item)

            def final_block(sc=sc, x1p=x1p, x1pn=x1pn):
                ln_rstd(btx[:, 8:12], btx[:, 12:16], btx[:, 0:4], btx[:, 4:8],
                        "btx", "btx", "btx", "btx")
                for t in range(4):
                    op("vector", lambda e, t=t: e.scalar_tensor_tensor(
                        out=u_s[:], in0=z_sb[:, D * t:D * t + D], scalar=0.5, in1=b2h_b[:],
                        op0=ALU.mult, op1=ALU.add),
                       reads=["z_sb", "b2h_b"], writes=["u_s"])
                    op("vector", lambda e, t=t, x1p=x1p: e.scalar_tensor_tensor(
                        out=r2_s[:], in0=u_s[:], scalar=1.0, in1=x1p[:, D * t:D * t + D],
                        op0=ALU.mult, op1=ALU.add),
                       reads=["u_s", x1pn], writes=["r2_s"])
                    op("scalar", lambda e, t=t: e.activation(
                        out=out_s[:], in_=r2_s[:], func=AF.Identity,
                        bias=btx[:, 4 + t:5 + t], scale=btx[:, t:t + 1]),
                       reads=["r2_s", "btx"], writes=["out_s"])
                    op("sync", lambda e, sc=sc, t=t: e.dma_start(
                        out=out_h.ap()[512 * sc + 128 * t: 512 * sc + 128 * t + 128, :], in_=out_s[:]),
                       dma=True, reads=["out_s"], writes=["out"])
            PENDING[0] = final_block

        def drain_b(n):
            for _ in range(min(n, len(B_QUEUE))):
                B_QUEUE.pop(0)()

        def deq_src_issue(cc):
            gq = cc // 4
            if cc % 4 == 0:
                gr, gn = (g1a, "g1a") if gq % 2 == 0 else (g1b, "g1b")
                s1, s1n = (s1b_a, "s1b_a") if gq % 2 == 0 else (s1b_b, "s1b_b")
                op("gpsimd", lambda e, gr=gr, gq=gq: e.dma_start(
                    out=gr[:], in_=fc1q_h.ap()[128 * gq:128 * gq + 128, :]),
                   dma=True, writes=[gn])
                op("sync", lambda e, s1=s1, gq=gq: e.dma_start(out=s1[:], in_=fc1sb_h.ap()[gq, :, :]),
                   dma=True, writes=[s1n])
            g2r, g2n = (g2a, "g2a") if cc % 2 == 0 else (g2b, "g2b")
            s2, s2n = (s2b_a, "s2b_a") if cc % 2 == 0 else (s2b_b, "s2b_b")
            op("gpsimd", lambda e, g2r=g2r, cc=cc: e.dma_start(
                out=g2r[:], in_=fc2q_h.ap()[128 * cc:128 * cc + 128, :]),
               dma=True, writes=[g2n])
            op("sync", lambda e, s2=s2, cc=cc: e.dma_start(out=s2[:], in_=fc2sb_h.ap()[cc, :, :]),
               dma=True, writes=[s2n])

        if n_pref > 0:
            x_load_pref(xp_h, 0)
            x_transpose_pipe(0)
            deq_src_issue(0)
        for c in range(n_pref):
            g = c % 2
            gq = c // 4
            if c + 1 < n_pref:
                x_load_pref(xp_h, c + 1)
                deq_src_issue(c + 1)
            kv_mms(g, *((x_fma, "x_fma") if g == 0 else (x_fmb, "x_fmb")))
            if c + 1 < n_pref:
                x_transpose_pipe(c + 1)
            if c > 0:
                state_update(1 - g, prefix=True)
            elu_acts(g)
            v_evac(g)
            khat_add()
            gr, gn = (g1a, "g1a") if gq % 2 == 0 else (g1b, "g1b")
            s1, s1n = (s1b_a, "s1b_a") if gq % 2 == 0 else (s1b_b, "s1b_b")
            g2r, g2n = (g2a, "g2a") if c % 2 == 0 else (g2b, "g2b")
            s2, s2n = (s2b_a, "s2b_a") if c % 2 == 0 else (s2b_b, "s2b_b")
            for k in range(2):
                sl = (c % 4) * 2 + k
                wsl, wsln = (wsl_a, "wsl_a") if k == 0 else (wsl_b, "wsl_b")
                outap = w1_d.ap().rearrange("(t p) n -> p t n", p=128)[
                    :, 32 * sl + gq: 32 * sl + gq + 25: 8, :]
                op("vector", lambda e, sl=sl, wsl=wsl, gr=gr, s1=s1: e.scalar_tensor_tensor(
                    out=wsl[:], in0=gr[:, 512 * sl:512 * sl + 512], scalar=-128.0,
                    in1=s1[:, 512 * sl:512 * sl + 512], op0=ALU.add, op1=ALU.mult),
                   reads=[gn, s1n], writes=[wsln])
                op("sync", lambda e, outap=outap, wsl=wsl: e.dma_start(
                    out=outap, in_=wsl.ap().rearrange("p (j n) -> p j n", j=4)),
                   dma=True, reads=[wsln], writes=["w1d"])
            for k in range(2):
                wsl, wsln = (wsl_c, "wsl_c") if k == 0 else (wsl_d, "wsl_d")
                dequant_slice(g2r, g2n, s2, s2n,
                              w2_d.ap()[128 * c:128 * c + 128, 512 * k:512 * k + 512],
                              "w2d", wsl, wsln, 512 * k)
        if n_pref > 0:
            state_update((n_pref - 1) % 2, prefix=True)
            op("vector", lambda e: e.tensor_copy(
                out=state_b.ap()[0:64, :].rearrange("p (c n) -> p c n", n=64)[:, 0:16:2, :],
                in_=state_f.ap()[0:64, :].rearrange("p (c n) -> p c n", n=64)),
               reads=["state_f"], writes=["state_b"])
            op("vector", lambda e: e.tensor_copy(
                out=state_b.ap()[64:128, :].rearrange("p (c n) -> p c n", n=64)[:, 1:16:2, :],
                in_=state_f.ap()[64:128, :].rearrange("p (c n) -> p c n", n=64)),
               reads=["state_f"], writes=["state_b"])
            op("vector", lambda e: e.tensor_copy(out=ksr_b[0:1, :], in_=ksr_f[:]),
               reads=["ksr_f"], writes=["ksr_b"])

        snap_a1 = S.snapshot()
        for engname in ("sync", "tensor", "vector", "scalar", "gpsimd"):
            S.wait_all(engname, snap_a1)

        op("vector", lambda e: e.memset(z_sb[:], 0.0), writes=["z_sb"])
        op("sync", lambda e: e.dma_start(out=z_sb[0:1, 0:D], in_=fc2b_h.ap()), dma=True, sem="sr", writes=["z_sb"])
        for (rowbuf, rowname, dst, dname, scale) in (
            (z_sb, "z_sb", b2h_b, "b2h_b", 0.5),
        ):
            for s in range(2):
                op("tensor", lambda e, rowbuf=rowbuf, s=s: e.matmul(
                    PS1[:, 512 * s:512 * s + 512], ones_ff[:], rowbuf[:, 512 * s:512 * s + 512],
                    start=True, stop=True),
                   reads=["ones_ff", rowname], writes=[bank(1, s)])
            op("scalar", lambda e, dst=dst, scale=scale: e.activation(
                out=dst[:], in_=PS1[:, 0:1024], func=AF.Copy, scale=scale),
               reads=["p1a", "p1b"], writes=[dname])

        for c in range(n_own):
            if c % 4 == 0 and c >= 4 and (c // 4 - 1) < n_b:
                drain_b(len(B_QUEUE))
                if PENDING[0] is not None:
                    PENDING[0]()
                    PENDING[0] = None
                emit_b_super(c // 4 - 1)
            rem = n_own - c
            budget = (len(B_QUEUE) + rem - 1) // rem if B_QUEUE else 0
            b3 = (budget + 1) // 2
            xbuf, xname = (xa, "xa") if c % 2 == 0 else (xb, "xb")
            if c == 0:
                x_load_transpose(xo_h, 0, xbuf, xname, 1)
            for d in range(8):
                for s in range(2):
                    op("tensor", lambda e, d=d, s=s: e.matmul(
                        PS1[:, 512 * s:512 * s + 512],
                        x_fma[:, 128 * d:128 * d + 128],
                        wqkv[:, 3072 * d + 512 * s: 3072 * d + 512 * s + 512],
                        start=(d == 0), stop=(d == 7)),
                       reads=["x_fma", f"wqkv{d}"], writes=[bank(1, s)])
            op("scalar", lambda e: e.activation(out=t1[:], in_=PS1[:, 0:1024], func=AF.Relu, scale=-1.0),
               reads=["p1a", "p1b"], writes=["t1"])
            op("scalar", lambda e: e.activation(out=t2[:], in_=t1[:], func=AF.Exp, scale=-1.0),
               reads=["t1"], writes=["t2"])
            op("scalar", lambda e: e.activation(out=t1[:], in_=PS1[:, 0:1024], func=AF.Relu),
               reads=["p1a", "p1b"], writes=["t1"])
            op("vector", lambda e: e.tensor_tensor(out=qhat[:], in0=t2[:], in1=t1[:], op=ALU.add),
               reads=["t2", "t1"], writes=["qhat"])
            kv_mms(0, x_fma, "x_fma")
            if c + 1 < n_own:
                xb2, xn2 = (xa, "xa") if (c + 1) % 2 == 0 else (xb, "xb")
                x_load_transpose(xo_h, c + 1, xb2, xn2, 1)
            elu_v(0)
            drain_b(b3)
            for d in range(8):
                op("tensor", lambda e, d=d: e.matmul(PS1[:, 1024 + 128 * d:1152 + 128 * d],
                                                     khat[:, 128 * d:128 * d + 128], ident[:],
                                                     start=True, stop=True),
                   reads=["khat", "ident"], writes=banksr(1, 1024 + 128 * d, 1152 + 128 * d))
            op("vector", lambda e: e.tensor_copy(
                out=k_fm.ap()[0:64, :].rearrange("p (cc n) -> p cc n", n=128)[:, 0:16:2, :],
                in_=PS1.ap()[0:64, 1024:2048].rearrange("p (cc n) -> p cc n", n=128)),
               reads=["p1c", "p1d"], writes=["k_fm"])
            op("vector", lambda e: e.tensor_copy(
                out=k_fm.ap()[64:128, :].rearrange("p (cc n) -> p cc n", n=128)[:, 1:16:2, :],
                in_=PS1.ap()[64:128, 1024:2048].rearrange("p (cc n) -> p cc n", n=128)),
               reads=["p1c", "p1d"], writes=["k_fm"])
            for s in range(2):
                bk = bank(1, s)
                op("tensor", lambda e, s=s: e.matmul(PS1[:, 512 * s:512 * s + 512], umask[:],
                                                     khat[:, 512 * s:512 * s + 512],
                                                     start=True, stop=False),
                   reads=["umask", "khat"], writes=[bk])
                op("tensor", lambda e, s=s: e.matmul(PS1[:, 512 * s:512 * s + 512], ones_fb[:],
                                                     ksr_b[:, 512 * s:512 * s + 512],
                                                     start=False, stop=True),
                   reads=["ones_fb", "ksr_b"], writes=[bk])
            drain_b(b3)
            op("vector", lambda e: e.tensor_tensor(out=t2[:], in0=PS1[:, 0:1024], in1=qhat[:], op=ALU.mult),
               reads=["p1a", "p1b", "qhat"], writes=["t2"])
            op("vector", lambda e: e.tensor_reduce(out=dr32[:, 0:16],
                                                   in_=t2.ap().rearrange("p (h d) -> p h d", d=DH),
                                                   axis=AX.X, op=ALU.add),
               reads=["t2"], writes=["dr32"])
            op("vector", lambda e: e.tensor_scalar_add(dr32[:, 16:32], dr32[:, 0:16], EPS_ATTN),
               reads=["dr32"], writes=["dr32"])
            op("vector", lambda e: e.reciprocal(out=dr32[:, 0:16], in_=dr32[:, 16:32]),
               reads=["dr32"], writes=["dr32"])
            op("vector", lambda e: e.tensor_tensor(
                out=qp_tm.ap().rearrange("p (h d) -> p h d", d=DH),
                in0=qhat.ap().rearrange("p (h d) -> p h d", d=DH),
                in1=dr32.ap()[:, 0:16].unsqueeze(2).to_broadcast([128, 16, DH]),
                op=ALU.mult),
               reads=["qhat", "dr32"], writes=["qp_tm"])
            for d in range(8):
                op("tensor", lambda e, d=d: e.matmul(PS1[:, 1024 + 128 * d:1152 + 128 * d],
                                                     qp_tm[:, 128 * d:128 * d + 128], ident[:],
                                                     start=True, stop=True),
                   reads=["qp_tm", "ident"], writes=banksr(1, 1024 + 128 * d, 1152 + 128 * d))
            op("scalar", lambda e: e.copy(out=qp_fm[:], in_=PS1[:, 1024:2048]),
               reads=["p1c", "p1d"], writes=["qp_fm"])
            for half in range(2):
                for h in range(8 * half, 8 * half + 8):
                    fo = 128 * (h // 2)
                    op("tensor", lambda e, h=h, fo=fo: e.matmul(
                        PS0[:, 128 * (h % 8):128 * (h % 8) + 128],
                        k_fm[:, 128 * h:128 * h + 128], qp_fm[:, fo:fo + 128],
                        start=True, stop=True),
                       reads=["k_fm", "qp_fm"], writes=banksr(0, 128 * (h % 8), 128 * (h % 8) + 128))
                op("vector", lambda e: e.tensor_tensor(
                    out=st_m.ap().rearrange("p (h t) -> p h t", t=128),
                    in0=PS0.ap()[:, 0:1024].rearrange("p (h t) -> p h t", t=128),
                    in1=umask.ap().unsqueeze(1).to_broadcast([128, 8, 128]),
                    op=ALU.mult),
                   reads=["p0a", "p0b", "umask"], writes=["st_m"])
                if half == 0:
                    drain_b(b3)
                for h in range(8 * half, 8 * half + 8):
                    pb, fo = 64 * (h % 2), 128 * (h // 2)
                    bk = "p1a" if h // 2 < 4 else "p1b"
                    op("tensor", lambda e, h=h, pb=pb, fo=fo: e.matmul(
                        PS1[pb:pb + 64, fo:fo + 128],
                        v_tm[:, 64 * h:64 * h + 64], st_m[:, 128 * (h % 8):128 * (h % 8) + 128],
                        start=True, stop=False),
                       reads=["v_tm", "st_m"], writes=[bk])
                    op("tensor", lambda e, h=h, pb=pb, fo=fo: e.matmul(
                        PS1[pb:pb + 64, fo:fo + 128],
                        state_b[:, 64 * h:64 * h + 64],
                        qp_fm[:, fo:fo + 128],
                        start=False, stop=True),
                       reads=["state_b", "qp_fm"], writes=[bk])
            op("scalar", lambda e: e.copy(out=attn[:], in_=PS1[:, 0:1024]),
               reads=["p1a", "p1b"], writes=["attn"])
            state_update(0, prefix=False)
            drain_b(b3)
            for d in range(8):
                for s in range(2):
                    op("tensor", lambda e, d=d, s=s: e.matmul(
                        PS1[:, 512 * s:512 * s + 512],
                        attn[:, 128 * d:128 * d + 128],
                        wo_s[:, 1024 * d + 512 * s: 1024 * d + 512 * s + 512],
                        start=(d == 0), stop=(d == 7)),
                       reads=["attn", f"wo{d}"], writes=[bank(1, s)])
            op("vector", lambda e, xbuf=xbuf: e.scalar_tensor_tensor(
                out=r_sb[:], in0=PS1[:, 0:1024], scalar=1.0, in1=xbuf[:],
                op0=ALU.mult, op1=ALU.add, accum_out=stx[:, 0:1]),
               reads=["p1a", "p1b", xname], writes=["r_sb", "stx"])
            op("vector", lambda e: e.scalar_tensor_tensor(
                out=t2[:], in0=r_sb[:], scalar=1.0, in1=r_sb[:],
                op0=ALU.mult, op1=ALU.mult, accum_out=stx[:, 1:2]),
               reads=["r_sb"], writes=["t2", "stx"])
            tq4 = c % 4
            x1p, x1pn = (x1_s, "x1_s") if (c // 4) % 2 == 0 else (x1_s2, "x1_s2")
            layer_norm_tail(r_sb[:], ["r_sb"], x1p[:, D * tq4:D * tq4 + D], x1pn,
                            stx[:, 0:1], stx[:, 1:2], stx[:, 2:3], stx[:, 3:4],
                            "stx", "stx", "stx", "stx",
                            acc_ap=bt1[:, tq4:tq4 + 1], acc_w=["bty"])
            op("vector", lambda e, tq4=tq4, x1p=x1p: e.scalar_tensor_tensor(
                out=u_s[:], in0=x1p[:, D * tq4:D * tq4 + D], scalar=1.0,
                in1=x1p[:, D * tq4:D * tq4 + D], op0=ALU.mult, op1=ALU.mult,
                accum_out=bt2[:, tq4:tq4 + 1]),
               reads=[x1pn], writes=["u_s", "bty"])
            drain_b(b3)
        if n_own == NCH and n_b == 8:
            drain_b(len(B_QUEUE))
            if PENDING[0] is not None:
                PENDING[0]()
                PENDING[0] = None
            emit_b_super(7)
        drain_b(len(B_QUEUE))
        if PENDING[0] is not None:
            PENDING[0]()
            PENDING[0] = None

        S.wait_all("sync", S.snapshot())

        with nc.Block() as blk:
            @blk.sync
            def _(eng):
                S.replay("sync", eng, sems)

            @blk.tensor
            def _(eng):
                S.replay("tensor", eng, sems)

            @blk.vector
            def _(eng):
                S.replay("vector", eng, sems)

            @blk.scalar
            def _(eng):
                S.replay("scalar", eng, sems)

            @blk.gpsimd
            def _(eng):
                S.replay("gpsimd", eng, sems)

    return nc


def kernel(L0_attn_x, L0_attn_Wq, L0_attn_Wk, L0_attn_Wv, L0_attn_Wo,
           L0_ffn_ln_gamma, L0_ffn_ln_beta, L0_ffn_fc1_q, L0_ffn_fc1_scales,
           L0_ffn_fc1_bias, L0_ffn_fc2_q, L0_ffn_fc2_scales, L0_ffn_fc2_bias):
    x = np.ascontiguousarray(np.asarray(L0_attn_x, dtype=np.float32))
    wqkvT = np.ascontiguousarray(
        np.concatenate([np.asarray(L0_attn_Wq).T, np.asarray(L0_attn_Wk).T,
                        np.asarray(L0_attn_Wv).T], axis=1).astype(np.float32))
    woT = np.ascontiguousarray(np.asarray(L0_attn_Wo).T.astype(np.float32))
    fc1q = np.ascontiguousarray(np.asarray(L0_ffn_fc1_q, dtype=np.int32))
    fc2q = np.ascontiguousarray(np.asarray(L0_ffn_fc2_q, dtype=np.int32))
    fc1s = np.ascontiguousarray(np.asarray(L0_ffn_fc1_scales, dtype=np.float32))
    fc2s = np.ascontiguousarray(np.asarray(L0_ffn_fc2_scales, dtype=np.float32))
    fc1sb = np.ascontiguousarray(np.broadcast_to(fc1s[:, None, :], (8, 128, F)))
    fc2sb = np.ascontiguousarray(np.broadcast_to(fc2s[:, None, :], (32, 128, D)))
    fc1b = np.ascontiguousarray(np.asarray(L0_ffn_fc1_bias, dtype=np.float32).reshape(32, 128).T)
    fc2b = np.ascontiguousarray(np.asarray(L0_ffn_fc2_bias, dtype=np.float32).reshape(1, D))
    gam = np.ascontiguousarray(np.asarray(L0_ffn_ln_gamma, dtype=np.float32).reshape(1, D))
    bet = np.ascontiguousarray(np.asarray(L0_ffn_ln_beta, dtype=np.float32).reshape(1, D))
    gamc = np.ascontiguousarray(np.asarray(L0_ffn_ln_gamma, dtype=np.float32).reshape(8, 128).T)
    betc = np.ascontiguousarray(np.asarray(L0_ffn_ln_beta, dtype=np.float32).reshape(8, 128).T)
    umaskI = np.triu(np.ones((128, 128), np.float32))
    identI = np.eye(128, dtype=np.float32)
    zeros_x = np.zeros((TOK, D), np.float32)

    nc = build_program()
    in_maps = []
    for c in range(8):
        b, j = c // 2, c % 2
        in_maps.append({
            "xo": np.ascontiguousarray(x[b, TOK * j:TOK * j + TOK]),
            "xp": np.ascontiguousarray(x[b, 0:TOK]) if j == 1 else zeros_x,
            "amask": np.array([[1.0 if j == 1 else 0.0]], np.float32),
            "wqkvT": wqkvT, "woT": woT,
            "fc1q": fc1q, "fc1s": fc1s, "fc1b": fc1b, "fc1sb": fc1sb, "fc2sb": fc2sb,
            "fc2q": fc2q, "fc2s": fc2s, "fc2b": fc2b,
            "gam": gam, "bet": bet, "gamC": gamc, "betC": betc,
            "umaskI": umaskI, "identI": identI,
        })
    res = run_bass_kernel_spmd(nc, in_maps, core_ids=list(range(8)))
    out = np.empty((4, 2 * TOK, D), np.float32)
    for c in range(8):
        b, j = c // 2, c % 2
        out[b, TOK * j:TOK * j + TOK] = np.asarray(res.results[c]["out"])
    if DEBUG:
        global DBG_RES
        DBG_RES = res.results
    return out


# version 26
# speedup vs baseline: 1.0202x; 1.0202x over previous
import numpy as np
import ml_dtypes
DBG_RES = None
import concourse.bass as bass
import concourse.mybir as mybir
import concourse.bass_utils as _bu
from concourse.bass_utils import run_bass_kernel_spmd
from contextlib import ExitStack

F32 = mybir.dt.float32
BF16 = mybir.dt.bfloat16
I32 = mybir.dt.int32
AF = mybir.ActivationFunctionType
ALU = mybir.AluOpType
AX = mybir.AxisListType
BF = ml_dtypes.bfloat16

_orig_bvo = _bu.bir_verify_and_optimise
def _bvo_fast(tmpdir, inp="bir.json", outp="file.neff", arch=None, *, dve_root=None):
    real_run = _bu.run_command
    def run_patched(cmd, cwd=None, **kw):
        cmd = ["--enable-birsim=false" if c == "--enable-birsim=true" else c for c in cmd]
        return real_run(cmd, cwd=cwd, **kw)
    _bu.run_command = run_patched
    try:
        return _orig_bvo(tmpdir, inp, outp, arch, dve_root=dve_root)
    finally:
        _bu.run_command = real_run
_bu.bir_verify_and_optimise = _bvo_fast

D = 1024
H = 16
DH = 64
TOK = 4096
NCH = 32
F = 4096
EPS_ATTN = 1e-8
EPS_LN = 1e-5

ENG_SEM = {"tensor": "pe", "vector": "dv", "scalar": "ac", "gpsimd": "po"}
DMA_SEM = {"sync": "sd", "gpsimd": "gd"}
DTSIZE = {F32: 4, BF16: 2, I32: 4}


class Sched:

    NLANES = 8

    def __init__(self):
        self.ops = {k: [] for k in ("sync", "tensor", "vector", "scalar", "gpsimd")}
        self.counts = {}
        for s in ("pe", "dv", "ac", "po"):
            self.counts[s] = 0
        for base in ("sd", "gd", "sr"):
            for l in range(self.NLANES):
                self.counts[base + str(l)] = 0
        self.lane = {"sd": 0, "gd": 0, "sr": 0}
        self.bufs = {}
        self.floor = {}

    def _mark(self, name):
        if name not in self.bufs:
            self.bufs[name] = {"w": None, "r": []}
        return self.bufs[name]

    def op(self, engine, build, reads=(), writes=(), dma=False, sem=None, label=""):
        if dma:
            base = sem or DMA_SEM[engine]
            sem = base + str(self.lane[base])
            self.lane[base] = (self.lane[base] + 1) % self.NLANES
            inc = 16
        else:
            sem = ENG_SEM[engine]
            inc = 1
        waits = {}
        def add_wait(mark):
            if mark is None:
                return
            s, v = mark
            if not dma and s == sem and engine == "tensor":
                return
            if waits.get(s, 0) < v:
                waits[s] = v
        for b in reads:
            add_wait(self._mark(b)["w"])
        for b in writes:
            m = self._mark(b)
            add_wait(m["w"])
            for rm in m["r"]:
                add_wait(rm)
        fl = []
        for s, v in waits.items():
            key = (engine, s)
            if self.floor.get(key, 0) < v:
                fl.append((s, v))
                self.floor[key] = v
        self.counts[sem] += inc
        mark = (sem, self.counts[sem])
        for b in reads:
            self._mark(b)["r"].append(mark)
        for b in writes:
            m = self._mark(b)
            m["w"] = mark
            m["r"] = []
        self.ops[engine].append((tuple(fl), build, sem, inc, label))
        return mark

    def wait_all(self, engine, marks):
        fl = []
        for s, v in marks:
            if s == ENG_SEM.get(engine):
                continue
            key = (engine, s)
            if self.floor.get(key, 0) < v:
                fl.append((s, v))
                self.floor[key] = v
        if fl:
            self.ops[engine].append((tuple(fl), None, None, 0, "barrier"))

    def snapshot(self):
        return [(s, v) for s, v in self.counts.items() if v > 0]

    def replay(self, engine, eng, sems):
        for fl, build, sem, inc, label in self.ops[engine]:
            for s, v in fl:
                eng.wait_ge(sems[s], v)
            if build is not None:
                build(eng).then_inc(sems[sem], inc)


STAGES = [NCH, NCH, 8]
DEBUG = False

def build_program():
    n_pref, n_own, n_b = STAGES
    nc = bass.Bass("TRN2", target_bir_lowering=False)

    xo_h = nc.dram_tensor("xo", [TOK, D], F32, kind="ExternalInput")
    xp_h = nc.dram_tensor("xp", [TOK, D], F32, kind="ExternalInput")
    amask_h = nc.dram_tensor("amask", [1, 1], F32, kind="ExternalInput")
    wqkv_h = nc.dram_tensor("wqkvT", [D, 3 * D], F32, kind="ExternalInput")
    wo_h = nc.dram_tensor("woT", [D, D], F32, kind="ExternalInput")
    fc1q_h = nc.dram_tensor("fc1q", [D, F], I32, kind="ExternalInput")
    fc1s_h = nc.dram_tensor("fc1s", [8, F], F32, kind="ExternalInput")
    fc1sb_h = nc.dram_tensor("fc1sb", [8, 128, F], F32, kind="ExternalInput")
    fc2sb_h = nc.dram_tensor("fc2sb", [32, 128, D], F32, kind="ExternalInput")
    fc1b_h = nc.dram_tensor("fc1b", [128, 32], F32, kind="ExternalInput")
    fc2q_h = nc.dram_tensor("fc2q", [F, D], I32, kind="ExternalInput")
    fc2s_h = nc.dram_tensor("fc2s", [32, D], F32, kind="ExternalInput")
    fc2b_h = nc.dram_tensor("fc2b", [1, D], F32, kind="ExternalInput")
    gam_h = nc.dram_tensor("gam", [1, D], F32, kind="ExternalInput")
    bet_h = nc.dram_tensor("bet", [1, D], F32, kind="ExternalInput")
    gamc_h = nc.dram_tensor("gamC", [128, 8], F32, kind="ExternalInput")
    betc_h = nc.dram_tensor("betC", [128, 8], F32, kind="ExternalInput")
    umask_h = nc.dram_tensor("umaskI", [128, 128], F32, kind="ExternalInput")
    ident_h = nc.dram_tensor("identI", [128, 128], F32, kind="ExternalInput")
    out_h = nc.dram_tensor("out", [TOK, D], F32, kind="ExternalOutput")
    dbg = "ExternalOutput" if DEBUG else "Internal"
    x1_d = nc.dram_tensor("x1d", [TOK, D], BF16, kind=dbg)
    w1_d = nc.dram_tensor("w1d", [256 * 128, 128], BF16, kind=dbg)
    w2_d = nc.dram_tensor("w2d", [F, D], BF16, kind=dbg)

    S = Sched()
    op = S.op

    with ExitStack() as ctx:
        def pers(name, shape, dtype):
            return ctx.enter_context(nc.sbuf_tensor(name, shape, dtype))

        state_f = pers("state_f", [128, 512], F32)
        state_b = pers("state_b", [128, 1024], BF16)
        ksr_f = pers("ksr_f", [1, D], F32)
        ksr_b = pers("ksr_b", [128, D], BF16)
        umask = pers("umask", [128, 128], BF16)
        ident = pers("ident", [128, 128], BF16)
        ones_fb = pers("ones_fb", [128, 128], BF16)
        ones_ff = pers("ones_ff", [128, 128], F32)
        ones_c = pers("ones_c", [128, 1], BF16)
        amask_s = pers("amask_s", [1, 1], F32)
        b1_s = pers("b1_s", [128, 32], F32)
        gamC = pers("gamC_s", [128, 8], F32)
        betC = pers("betC_s", [128, 8], F32)
        stx = pers("stx", [128, 4], F32)
        bty = pers("bty", [128, 16], F32)
        bt1 = bty.ap()[:, 0:4]
        bt2 = bty.ap()[:, 4:8]
        bt4 = bty.ap()[:, 8:12]
        bt5 = bty.ap()[:, 12:16]
        btx = pers("btx", [128, 16], F32)

        arena_base = (nc.sbuf_base + 63) & ~63
        arena_top = nc.sbuf_top

        def make_arena(base):
            cur = [base]
            def alloc(name, shape, dtype):
                nbytes = shape[1] * DTSIZE[dtype]
                off = cur[0]
                cur[0] = (off + nbytes + 63) & ~63
                assert cur[0] <= arena_top, f"sbuf arena overflow at {name}: {cur[0]} > {arena_top}"
                return nc.alloc_sbuf_tensor_at(name, shape, dtype, offset=off)
            return alloc, cur

        A, acur = make_arena(arena_base)
        wqkv = A("wqkv", [128, 8 * 3 * D], BF16)
        wo_s = A("wo_s", [128, 8 * D], BF16)
        xa = A("xa", [128, D], BF16)
        xb = A("xb", [128, D], BF16)
        x_fma = A("x_fma", [128, D], BF16)
        x_fmb = A("x_fmb", [128, D], BF16)
        t1 = A("t1", [128, D], BF16)
        t2 = A("t2", [128, D], BF16)
        qhat = A("qhat", [128, D], BF16)
        khat = A("khat", [128, D], BF16)
        v_tm = A("v_tm", [128, D], BF16)
        k_fm = A("k_fm", [128, 2 * D], BF16)
        qp_tm = A("qp_tm", [128, D], BF16)
        qp_fm = A("qp_fm", [128, D], BF16)
        st_m = A("st_m", [128, 1024], BF16)
        attn = A("attn", [128, D], BF16)
        r_sb = A("r_sb", [128, D], F32)
        dr32 = A("dr32", [128, 32], F32)
        g1base = acur[0]
        G, gcur = make_arena(g1base)
        g1a = G("g1a", [128, F], F32)
        g1b = G("g1b", [128, F], F32)
        g2a = G("g2a", [128, D], F32)
        g2b = G("g2b", [128, D], F32)
        wsl_a = G("wsl_a", [128, 512], BF16)
        wsl_b = G("wsl_b", [128, 512], BF16)
        wsl_c = G("wsl_c", [128, 512], BF16)
        wsl_d = G("wsl_d", [128, 512], BF16)
        s1b_a = G("s1b_a", [128, F], F32)
        s1b_b = G("s1b_b", [128, F], F32)
        s2b_a = G("s2b_a", [128, D], F32)
        s2b_b = G("s2b_b", [128, D], F32)
        B, bcur = make_arena(g1base)
        w1r_a = B("w1r_a", [128, 4096], BF16)
        w1r_b = B("w1r_b", [128, 4096], BF16)
        w2r_a = B("w2r_a", [128, 4096], BF16)
        w2r_b = B("w2r_b", [128, 4096], BF16)
        y_fm = B("y_fm", [128, 8 * 512], BF16)
        x1_s = B("x1_s", [128, 4 * D], BF16)
        x1_s2 = B("x1_s2", [128, 4 * D], BF16)
        h_fm = B("h_fm", [128, 4 * D], BF16)
        u_bf_a = B("u_bf_a", [128, D], BF16)
        u_bf_b = B("u_bf_b", [128, D], BF16)
        b2h_b = B("b2h_b", [128, D], BF16)
        u_s = B("u_s", [128, D], F32)
        r2_s = B("r2_s", [128, D], F32)
        z_sb = B("z_sb", [128, 4 * D], F32)
        out_s = B("out_s", [128, D], F32)

        PS0 = ctx.enter_context(nc.psum_tensor("PS0", [128, 2048], F32))
        PS1 = ctx.enter_context(nc.psum_tensor("PS1", [128, 2048], F32))
        PS = [PS0, PS1]
        def bank(g, k):
            return f"p{g}{'abcd'[k]}"
        def banksr(g, lo, hi):
            return [bank(g, k) for k in range(lo // 512, (hi + 511) // 512)]

        semnames = ["pe", "dv", "ac", "po"] + [b + str(l) for b in ("sd", "gd", "sr")
                                               for l in range(Sched.NLANES)]
        sems = {n: ctx.enter_context(nc.semaphore(n)) for n in semnames}

        op("vector", lambda e: e.memset(state_f[:], 0.0), writes=["state_f"])
        op("vector", lambda e: e.memset(state_b[:], 0.0), writes=["state_b"])
        op("vector", lambda e: e.memset(ksr_f[:], 0.0), writes=["ksr_f"])
        op("vector", lambda e: e.memset(ksr_b[:], 0.0), writes=["ksr_b"])
        op("vector", lambda e: e.memset(ones_fb[:], 1.0), writes=["ones_fb"])
        op("vector", lambda e: e.memset(ones_ff[:], 1.0), writes=["ones_ff"])
        op("vector", lambda e: e.memset(ones_c[:], 1.0), writes=["ones_c"])
        op("vector", lambda e: e.memset(k_fm[:], 0.0), writes=["k_fm"])
        op("sync", lambda e: e.dma_start(out=amask_s[:], in_=amask_h.ap()), dma=True, sem="sr", writes=["amask_s"])
        op("sync", lambda e: e.dma_start(out=b1_s[:], in_=fc1b_h.ap()), dma=True, writes=["b1_s"])
        op("sync", lambda e: e.dma_start(out=gamC[:], in_=gamc_h.ap()), dma=True, writes=["gamC"])
        op("sync", lambda e: e.dma_start(out=betC[:], in_=betc_h.ap()), dma=True, writes=["betC"])
        op("gpsimd", lambda e: e.dma_start(out=umask[:], in_=umask_h.ap()), dma=True, writes=["umask"])
        op("gpsimd", lambda e: e.dma_start(out=ident[:], in_=ident_h.ap()), dma=True, writes=["ident"])
        wq_in = wqkv_h.ap().rearrange("(c p) n -> p c n", p=128)
        for dd in range(8):
            op("gpsimd", lambda e, dd=dd: e.dma_start(out=wqkv[:, 3072 * dd:3072 * dd + 3072],
                                                      in_=wq_in[:, dd, :]),
               dma=True, writes=[f"wqkv{dd}"])
        wo_in = wo_h.ap().rearrange("(c p) n -> p c n", p=128)
        for dd in range(8):
            op("gpsimd", lambda e, dd=dd: e.dma_start(out=wo_s[:, 1024 * dd:1024 * dd + 1024],
                                                      in_=wo_in[:, dd, :]),
               dma=True, writes=[f"wo{dd}"])

        def x_load_transpose(src_h, c, xbuf, xname, tg):
            op("gpsimd", lambda e: e.dma_start(out=xbuf[:], in_=src_h.ap()[128 * c:128 * c + 128, :]),
               dma=True, writes=[xname])
            for d in range(8):
                op("tensor", lambda e, d=d: e.matmul(PS[tg][:, 1024 + 128 * d:1152 + 128 * d],
                                                     xbuf[:, 128 * d:128 * d + 128], ident[:],
                                                     start=True, stop=True),
                   reads=[xname, "ident"], writes=banksr(tg, 1024 + 128 * d, 1152 + 128 * d))
            op("scalar", lambda e: e.copy(out=x_fma[:], in_=PS[tg][:, 1024:2048]),
               reads=[bank(tg, 2), bank(tg, 3)], writes=["x_fma"])

        def x_load_pref(src_h, c):
            xbuf, xname = (xa, "xa") if c % 2 == 0 else (xb, "xb")
            op("gpsimd", lambda e, xbuf=xbuf, c=c: e.dma_start(
                out=xbuf[:], in_=src_h.ap()[128 * c:128 * c + 128, :]),
               dma=True, writes=[xname])

        def x_transpose_pipe(c):
            j = c % 2
            xbuf, xname = (xa, "xa") if c % 2 == 0 else (xb, "xb")
            xfm, xfmn = (x_fma, "x_fma") if j == 0 else (x_fmb, "x_fmb")
            for w in range(2):
                for d in range(4):
                    dd = 4 * w + d
                    op("tensor", lambda e, dd=dd, d=d, xbuf=xbuf: e.matmul(
                        PS[j][:, 1536 + 128 * d:1664 + 128 * d],
                        xbuf[:, 128 * dd:128 * dd + 128], ident[:],
                        start=True, stop=True),
                       reads=[xname, "ident"], writes=[bank(j, 3)])
                op("vector", lambda e, w=w, xfm=xfm: e.tensor_copy(
                    out=xfm[:, 512 * w:512 * w + 512], in_=PS[j][:, 1536:2048]),
                   reads=[bank(j, 3)], writes=[xfmn])

        def kv_mms(g, xfm, xfmn):
            for d in range(8):
                for s in range(4):
                    op("tensor", lambda e, d=d, s=s, xfm=xfm: e.matmul(
                        PS[g][:, 512 * s:512 * s + 512],
                        xfm[:, 128 * d:128 * d + 128],
                        wqkv[:, 3072 * d + D + 512 * s: 3072 * d + D + 512 * s + 512],
                        start=(d == 0), stop=(d == 7)),
                       reads=[xfmn, f"wqkv{d}"], writes=[bank(g, s)])

        def elu_acts(g):
            op("scalar", lambda e: e.activation(out=t1[:], in_=PS[g][:, 0:1024], func=AF.Relu, scale=-1.0),
               reads=[bank(g, 0), bank(g, 1)], writes=["t1"])
            op("scalar", lambda e: e.activation(out=t2[:], in_=t1[:], func=AF.Exp, scale=-1.0),
               reads=["t1"], writes=["t2"])
            op("scalar", lambda e: e.activation(out=t1[:], in_=PS[g][:, 0:1024], func=AF.Relu),
               reads=[bank(g, 0), bank(g, 1)], writes=["t1"])

        def khat_add():
            op("vector", lambda e: e.tensor_tensor(out=khat[:], in0=t2[:], in1=t1[:], op=ALU.add),
               reads=["t2", "t1"], writes=["khat"])

        def v_evac(g):
            op("vector", lambda e: e.tensor_copy(out=v_tm[:], in_=PS[g][:, 1024:2048]),
               reads=[bank(g, 2), bank(g, 3)], writes=["v_tm"])

        def elu_v(g):
            elu_acts(g)
            khat_add()
            v_evac(g)

        def state_update(g, prefix):
            for h in range(16):
                op("tensor", lambda e, h=h: e.matmul(
                    PS[g][64 * (h % 2):64 * (h % 2) + 64, 64 * (h // 2):64 * (h // 2) + 64],
                    khat[:, 64 * h:64 * h + 64], v_tm[:, 64 * h:64 * h + 64],
                    start=True, stop=True),
                   reads=["khat", "v_tm"], writes=[bank(g, 0)])
            op("vector", lambda e: e.tensor_tensor(out=state_f[:], in0=PS[g][:, 0:512], in1=state_f[:], op=ALU.add),
               reads=[bank(g, 0), "state_f"], writes=["state_f"])
            if not prefix:
                op("vector", lambda e: e.tensor_copy(
                    out=state_b.ap()[0:64, :].rearrange("p (c n) -> p c n", n=64)[:, 0:16:2, :],
                    in_=state_f.ap()[0:64, :].rearrange("p (c n) -> p c n", n=64)),
                   reads=["state_f"], writes=["state_b"])
                op("vector", lambda e: e.tensor_copy(
                    out=state_b.ap()[64:128, :].rearrange("p (c n) -> p c n", n=64)[:, 1:16:2, :],
                    in_=state_f.ap()[64:128, :].rearrange("p (c n) -> p c n", n=64)),
                   reads=["state_f"], writes=["state_b"])
            for s in range(2):
                bk = bank(g, 1 + s)
                lo = 512 + 512 * s
                op("tensor", lambda e, s=s, lo=lo: e.matmul(
                    PS[g][0:1, lo:lo + 512], ones_c[:], khat[:, 512 * s:512 * s + 512],
                    start=True, stop=True),
                   reads=["ones_c", "khat"], writes=[bk])
                if prefix:
                    op("vector", lambda e, s=s, lo=lo: e.scalar_tensor_tensor(
                        out=ksr_f[0:1, 512 * s:512 * s + 512], in0=PS[g][0:1, lo:lo + 512],
                        scalar=amask_s[0:1, :], in1=ksr_f[0:1, 512 * s:512 * s + 512],
                        op0=ALU.mult, op1=ALU.add),
                       reads=[bk, "ksr_f", "amask_s"], writes=["ksr_f"])
                else:
                    op("vector", lambda e, s=s, lo=lo: e.tensor_tensor(
                        out=ksr_f[0:1, 512 * s:512 * s + 512], in0=PS[g][0:1, lo:lo + 512],
                        in1=ksr_f[0:1, 512 * s:512 * s + 512], op=ALU.add),
                       reads=[bk, "ksr_f"], writes=["ksr_f"])
            if not prefix:
                op("vector", lambda e: e.tensor_copy(out=ksr_b[0:1, :], in_=ksr_f[:]),
                   reads=["ksr_f"], writes=["ksr_b"])

        def dequant_slice(qgran, gname, scltile, sclname, outap, wd_name, wsl, wslname, scol):
            op("vector", lambda e: e.scalar_tensor_tensor(
                out=wsl[:], in0=qgran[:, scol:scol + 512], scalar=-128.0,
                in1=scltile[:, scol:scol + 512], op0=ALU.add, op1=ALU.mult),
               reads=[gname, sclname], writes=[wslname])
            op("sync", lambda e: e.dma_start(out=outap, in_=wsl[:]),
               dma=True, reads=[wslname], writes=[wd_name])

        def layer_norm_tail(src, src_reads, outap, outname, s1, s2, s4, s5, n1, n2, n4, n5,
                            acc_ap=None, acc_w=[]):
            op("vector", lambda e: e.tensor_scalar_mul(s1, s1, 1.0 / D),
               reads=[n1], writes=[n1])
            op("vector", lambda e: e.tensor_tensor(out=s5, in0=s1, in1=s1, op=ALU.mult),
               reads=[n1], writes=[n5])
            op("vector", lambda e: e.scalar_tensor_tensor(
                out=s2, in0=s2, scalar=1.0 / D, in1=s5, op0=ALU.mult, op1=ALU.subtract),
               reads=[n2, n5], writes=[n2])
            op("vector", lambda e: e.tensor_scalar_add(s2, s2, EPS_LN),
               reads=[n2], writes=[n2])
            op("scalar", lambda e: e.activation(out=s4, in_=s2, func=AF.Ln),
               reads=[n2], writes=[n4])
            op("scalar", lambda e: e.activation(out=s4, in_=s4, func=AF.Exp, scale=-0.5),
               reads=[n4], writes=[n4])
            op("vector", lambda e: e.scalar_tensor_tensor(
                out=s5, in0=s1, scalar=-1.0, in1=s4, op0=ALU.mult, op1=ALU.mult),
               reads=[n1, n4], writes=[n5])
            op("vector", lambda e: e.scalar_tensor_tensor(
                out=outap, in0=src, scalar=s4, in1=s5.to_broadcast([128, D]),
                op0=ALU.mult, op1=ALU.add, accum_out=acc_ap),
               reads=list(src_reads) + [n4, n5], writes=[outname] + acc_w)

        B_QUEUE = []

        def bq(fn):
            B_QUEUE.append(fn)

        def ln_rstd(b1, b2, b4, b5, n1, n2, n4, n5):
            op("vector", lambda e: e.tensor_scalar_mul(b1, b1, 1.0 / D),
               reads=[n1], writes=[n1])
            op("vector", lambda e: e.tensor_tensor(out=b5, in0=b1, in1=b1, op=ALU.mult),
               reads=[n1], writes=[n5])
            op("vector", lambda e: e.scalar_tensor_tensor(
                out=b2, in0=b2, scalar=1.0 / D, in1=b5, op0=ALU.mult, op1=ALU.subtract),
               reads=[n2, n5], writes=[n2])
            op("vector", lambda e: e.tensor_scalar_add(b2, b2, EPS_LN),
               reads=[n2], writes=[n2])
            op("scalar", lambda e: e.activation(out=b4, in_=b2, func=AF.Ln),
               reads=[n2], writes=[n4])
            op("scalar", lambda e: e.activation(out=b4, in_=b4, func=AF.Exp, scale=-0.5),
               reads=[n4], writes=[n4])
            op("vector", lambda e: e.scalar_tensor_tensor(
                out=b5, in0=b1, scalar=-1.0, in1=b4, op0=ALU.mult, op1=ALU.mult),
               reads=[n1, n4], writes=[n5])

        PENDING = [None]

        def emit_b_super(sc):
            x1p, x1pn = (x1_s, "x1_s") if sc % 2 == 0 else (x1_s2, "x1_s2")
            ln_rstd(bt1, bt2, bt4, bt5, "bty", "bty", "bty", "bty")
            def apply_prep(t):
                ub, ubn = (u_bf_a, "u_bf_a") if t % 2 == 0 else (u_bf_b, "u_bf_b")
                def f(t=t, ub=ub, ubn=ubn):
                    op("scalar", lambda e, t=t, ub=ub: e.activation(
                        out=ub[:], in_=x1p[:, D * t:D * t + D], func=AF.Identity,
                        bias=bt5[:, t:t + 1], scale=bt4[:, t:t + 1]),
                       reads=[x1pn, "bty"], writes=[ubn])
                return f

            def apply_pe(t):
                ub, ubn = (u_bf_a, "u_bf_a") if t % 2 == 0 else (u_bf_b, "u_bf_b")
                def f(t=t, ub=ub, ubn=ubn):
                    for d in range(8):
                        op("tensor", lambda e, d=d, ub=ub: e.matmul(
                            PS1[:, 1024 + 128 * d:1152 + 128 * d],
                            ub[:, 128 * d:128 * d + 128], ident[:],
                            start=True, stop=True),
                           reads=[ubn, "ident"], writes=banksr(1, 1024 + 128 * d, 1152 + 128 * d))
                    for d in range(8):
                        op("scalar", lambda e, d=d, t=t: e.activation(
                            out=h_fm[:, 512 * d + 128 * t:512 * d + 128 * t + 128],
                            in_=PS1[:, 1024 + 128 * d:1152 + 128 * d], func=AF.Identity,
                            bias=betC[:, d:d + 1], scale=gamC[:, d:d + 1]),
                           reads=[bank(1, 2 + d // 4), "gamC", "betC"], writes=["h_fm"])
                return f
            bq(apply_prep(0))
            bq(apply_prep(1))
            bq(apply_pe(0))
            bq(apply_prep(2))
            bq(apply_pe(1))
            bq(apply_prep(3))
            bq(apply_pe(2))
            bq(apply_pe(3))
            def w1_load(mg):
                w1r, w1rn = (w1r_a, "w1r_a") if mg % 2 == 0 else (w1r_b, "w1r_b")
                def f(mg=mg, w1r=w1r, w1rn=w1rn):
                    op("sync", lambda e: e.dma_start(
                        out=w1r.ap().rearrange("p (t n) -> p t n", t=32),
                        in_=w1_d.ap().rearrange("(t p) n -> p t n", p=128)[:, 32 * mg:32 * mg + 32, :]),
                       dma=True, reads=["w1d"], writes=[w1rn])
                return f
            bq(w1_load(0))
            bq(w1_load(1))
            for q in range(4):
                def w2_loads(q=q):
                    for j, (buf, bn) in enumerate(((w2r_a, "w2r_a"), (w2r_b, "w2r_b"))):
                        e8 = 2 * q + j
                        op("sync", lambda e, buf=buf, e8=e8: e.dma_start(
                            out=buf.ap().rearrange("p (m n) -> p m n", m=4),
                            in_=w2_d.ap().rearrange("(m p) n -> p m n", p=128)[:, 4 * e8:4 * e8 + 4, :]),
                           dma=True, reads=["w2d"], writes=[bn])
                bq(w2_loads)
                for gi in range(2):
                    mg = 2 * q + gi
                    w1r, w1rn = (w1r_a, "w1r_a") if mg % 2 == 0 else (w1r_b, "w1r_b")
                    def fc1_item(mg=mg, w1r=w1r, w1rn=w1rn):
                        for mm in range(4):
                            m = 4 * mg + mm
                            yb = bank(0, m % 2)
                            lo = 512 * (m % 2)
                            for d in range(8):
                                op("tensor", lambda e, mm=mm, d=d, lo=lo, w1r=w1r: e.matmul(
                                    PS0[:, lo:lo + 512],
                                    w1r[:, 128 * (8 * mm + d):128 * (8 * mm + d) + 128],
                                    h_fm[:, 512 * d:512 * d + 512],
                                    start=(d == 0), stop=(d == 7)),
                                   reads=[w1rn, "h_fm"], writes=[yb])
                            op("scalar", lambda e, m=m, lo=lo: e.activation(
                                out=y_fm[:, 512 * (m % 8):512 * (m % 8) + 512], in_=PS0[:, lo:lo + 512],
                                func=AF.Silu, bias=b1_s[:, m:m + 1]),
                               reads=[yb, "b1_s"], writes=["y_fm"])
                    bq(fc1_item)
                    if mg + 2 < 8:
                        bq(w1_load(mg + 2))
                for t in range(4):
                    def fc2_item(q=q, t=t, sc=sc):
                        for m in range(8 * q, 8 * q + 8):
                            rbuf, rname = (w2r_a, "w2r_a") if (m // 4) % 2 == 0 else (w2r_b, "w2r_b")
                            for s in range(2):
                                op("tensor", lambda e, m=m, s=s, rbuf=rbuf: e.matmul(
                                    PS0[:, 1024 + 512 * s:1536 + 512 * s],
                                    y_fm[:, 512 * (m % 8) + 128 * t: 512 * (m % 8) + 128 * t + 128],
                                    rbuf[:, 1024 * (m % 4) + 512 * s: 1024 * (m % 4) + 512 * s + 512],
                                    start=(m % 8 == 0), stop=(m % 8 == 7)),
                                   reads=["y_fm", rname], writes=[bank(0, 2 + s)])
                        if q == 0:
                            op("vector", lambda e, t=t: e.tensor_copy(
                                out=z_sb[:, D * t:D * t + D], in_=PS0[:, 1024:2048]),
                               reads=["p0c", "p0d"], writes=["z_sb"])
                        else:
                            op("vector", lambda e, t=t: e.tensor_tensor(
                                out=z_sb[:, D * t:D * t + D], in0=PS0[:, 1024:2048],
                                in1=z_sb[:, D * t:D * t + D], op=ALU.add),
                               reads=["p0c", "p0d", "z_sb"], writes=["z_sb"])
                        if q == 3:
                            op("vector", lambda e, t=t: e.scalar_tensor_tensor(
                                out=u_s[:], in0=z_sb[:, D * t:D * t + D], scalar=0.5, in1=b2h_b[:],
                                op0=ALU.mult, op1=ALU.add),
                               reads=["z_sb", "b2h_b"], writes=["u_s"])
                            op("vector", lambda e, t=t: e.scalar_tensor_tensor(
                                out=r2_s[:], in0=u_s[:], scalar=1.0, in1=x1p[:, D * t:D * t + D],
                                op0=ALU.mult, op1=ALU.add, accum_out=btx[:, 8 + t:9 + t]),
                               reads=["u_s", x1pn], writes=["r2_s", "btx"])
                            op("vector", lambda e: e.scalar_tensor_tensor(
                                out=u_s[:], in0=r2_s[:], scalar=1.0, in1=r2_s[:],
                                op0=ALU.mult, op1=ALU.mult, accum_out=btx[:, 12 + t:13 + t]),
                               reads=["r2_s"], writes=["u_s", "btx"])
                    bq(fc2_item)

            def final_block(sc=sc, x1p=x1p, x1pn=x1pn):
                ln_rstd(btx[:, 8:12], btx[:, 12:16], btx[:, 0:4], btx[:, 4:8],
                        "btx", "btx", "btx", "btx")
                for t in range(4):
                    op("vector", lambda e, t=t: e.scalar_tensor_tensor(
                        out=u_s[:], in0=z_sb[:, D * t:D * t + D], scalar=0.5, in1=b2h_b[:],
                        op0=ALU.mult, op1=ALU.add),
                       reads=["z_sb", "b2h_b"], writes=["u_s"])
                    op("vector", lambda e, t=t, x1p=x1p: e.scalar_tensor_tensor(
                        out=r2_s[:], in0=u_s[:], scalar=1.0, in1=x1p[:, D * t:D * t + D],
                        op0=ALU.mult, op1=ALU.add),
                       reads=["u_s", x1pn], writes=["r2_s"])
                    op("scalar", lambda e, t=t: e.activation(
                        out=out_s[:], in_=r2_s[:], func=AF.Identity,
                        bias=btx[:, 4 + t:5 + t], scale=btx[:, t:t + 1]),
                       reads=["r2_s", "btx"], writes=["out_s"])
                    op("sync", lambda e, sc=sc, t=t: e.dma_start(
                        out=out_h.ap()[512 * sc + 128 * t: 512 * sc + 128 * t + 128, :], in_=out_s[:]),
                       dma=True, reads=["out_s"], writes=["out"])
            PENDING[0] = final_block

        def drain_b(n):
            for _ in range(min(n, len(B_QUEUE))):
                B_QUEUE.pop(0)()

        def deq_src_issue(cc):
            gq = cc // 4
            if cc % 4 == 0:
                gr, gn = (g1a, "g1a") if gq % 2 == 0 else (g1b, "g1b")
                s1, s1n = (s1b_a, "s1b_a") if gq % 2 == 0 else (s1b_b, "s1b_b")
                op("gpsimd", lambda e, gr=gr, gq=gq: e.dma_start(
                    out=gr[:], in_=fc1q_h.ap()[128 * gq:128 * gq + 128, :]),
                   dma=True, writes=[gn])
                op("sync", lambda e, s1=s1, gq=gq: e.dma_start(out=s1[:], in_=fc1sb_h.ap()[gq, :, :]),
                   dma=True, writes=[s1n])
            g2r, g2n = (g2a, "g2a") if cc % 2 == 0 else (g2b, "g2b")
            s2, s2n = (s2b_a, "s2b_a") if cc % 2 == 0 else (s2b_b, "s2b_b")
            op("gpsimd", lambda e, g2r=g2r, cc=cc: e.dma_start(
                out=g2r[:], in_=fc2q_h.ap()[128 * cc:128 * cc + 128, :]),
               dma=True, writes=[g2n])
            op("sync", lambda e, s2=s2, cc=cc: e.dma_start(out=s2[:], in_=fc2sb_h.ap()[cc, :, :]),
               dma=True, writes=[s2n])

        if n_pref > 0:
            x_load_pref(xp_h, 0)
            x_transpose_pipe(0)
            deq_src_issue(0)
        for c in range(n_pref):
            g = c % 2
            gq = c // 4
            if c + 1 < n_pref:
                x_load_pref(xp_h, c + 1)
                deq_src_issue(c + 1)
            kv_mms(g, *((x_fma, "x_fma") if g == 0 else (x_fmb, "x_fmb")))
            if c + 1 < n_pref:
                x_transpose_pipe(c + 1)
            if c > 0:
                state_update(1 - g, prefix=True)
            elu_acts(g)
            v_evac(g)
            khat_add()
            gr, gn = (g1a, "g1a") if gq % 2 == 0 else (g1b, "g1b")
            s1, s1n = (s1b_a, "s1b_a") if gq % 2 == 0 else (s1b_b, "s1b_b")
            g2r, g2n = (g2a, "g2a") if c % 2 == 0 else (g2b, "g2b")
            s2, s2n = (s2b_a, "s2b_a") if c % 2 == 0 else (s2b_b, "s2b_b")
            for k in range(2):
                sl = (c % 4) * 2 + k
                wsl, wsln = (wsl_a, "wsl_a") if k == 0 else (wsl_b, "wsl_b")
                outap = w1_d.ap().rearrange("(t p) n -> p t n", p=128)[
                    :, 32 * sl + gq: 32 * sl + gq + 25: 8, :]
                op("vector", lambda e, sl=sl, wsl=wsl, gr=gr, s1=s1: e.scalar_tensor_tensor(
                    out=wsl[:], in0=gr[:, 512 * sl:512 * sl + 512], scalar=-128.0,
                    in1=s1[:, 512 * sl:512 * sl + 512], op0=ALU.add, op1=ALU.mult),
                   reads=[gn, s1n], writes=[wsln])
                op("sync", lambda e, outap=outap, wsl=wsl: e.dma_start(
                    out=outap, in_=wsl.ap().rearrange("p (j n) -> p j n", j=4)),
                   dma=True, reads=[wsln], writes=["w1d"])
            for k in range(2):
                wsl, wsln = (wsl_c, "wsl_c") if k == 0 else (wsl_d, "wsl_d")
                dequant_slice(g2r, g2n, s2, s2n,
                              w2_d.ap()[128 * c:128 * c + 128, 512 * k:512 * k + 512],
                              "w2d", wsl, wsln, 512 * k)
        if n_pref > 0:
            state_update((n_pref - 1) % 2, prefix=True)
            op("vector", lambda e: e.tensor_copy(
                out=state_b.ap()[0:64, :].rearrange("p (c n) -> p c n", n=64)[:, 0:16:2, :],
                in_=state_f.ap()[0:64, :].rearrange("p (c n) -> p c n", n=64)),
               reads=["state_f"], writes=["state_b"])
            op("vector", lambda e: e.tensor_copy(
                out=state_b.ap()[64:128, :].rearrange("p (c n) -> p c n", n=64)[:, 1:16:2, :],
                in_=state_f.ap()[64:128, :].rearrange("p (c n) -> p c n", n=64)),
               reads=["state_f"], writes=["state_b"])
            op("vector", lambda e: e.tensor_copy(out=ksr_b[0:1, :], in_=ksr_f[:]),
               reads=["ksr_f"], writes=["ksr_b"])

        snap_a1 = S.snapshot()
        for engname in ("sync", "tensor", "vector", "scalar", "gpsimd"):
            S.wait_all(engname, snap_a1)

        op("vector", lambda e: e.memset(z_sb[:], 0.0), writes=["z_sb"])
        op("sync", lambda e: e.dma_start(out=z_sb[0:1, 0:D], in_=fc2b_h.ap()), dma=True, sem="sr", writes=["z_sb"])
        for (rowbuf, rowname, dst, dname, scale) in (
            (z_sb, "z_sb", b2h_b, "b2h_b", 0.5),
        ):
            for s in range(2):
                op("tensor", lambda e, rowbuf=rowbuf, s=s: e.matmul(
                    PS1[:, 512 * s:512 * s + 512], ones_ff[:], rowbuf[:, 512 * s:512 * s + 512],
                    start=True, stop=True),
                   reads=["ones_ff", rowname], writes=[bank(1, s)])
            op("scalar", lambda e, dst=dst, scale=scale: e.activation(
                out=dst[:], in_=PS1[:, 0:1024], func=AF.Copy, scale=scale),
               reads=["p1a", "p1b"], writes=[dname])

        for c in range(n_own):
            if c % 4 == 0 and c >= 4 and (c // 4 - 1) < n_b:
                drain_b(len(B_QUEUE))
                if PENDING[0] is not None:
                    PENDING[0]()
                    PENDING[0] = None
                emit_b_super(c // 4 - 1)
            rem = n_own - c
            budget = (len(B_QUEUE) + rem - 1) // rem if B_QUEUE else 0
            b3 = (budget + 2) // 3
            xbuf, xname = (xa, "xa") if c % 2 == 0 else (xb, "xb")
            if c == 0:
                x_load_transpose(xo_h, 0, xbuf, xname, 1)
            for d in range(8):
                for s in range(2):
                    op("tensor", lambda e, d=d, s=s: e.matmul(
                        PS1[:, 512 * s:512 * s + 512],
                        x_fma[:, 128 * d:128 * d + 128],
                        wqkv[:, 3072 * d + 512 * s: 3072 * d + 512 * s + 512],
                        start=(d == 0), stop=(d == 7)),
                       reads=["x_fma", f"wqkv{d}"], writes=[bank(1, s)])
            op("scalar", lambda e: e.activation(out=t1[:], in_=PS1[:, 0:1024], func=AF.Relu, scale=-1.0),
               reads=["p1a", "p1b"], writes=["t1"])
            op("scalar", lambda e: e.activation(out=t2[:], in_=t1[:], func=AF.Exp, scale=-1.0),
               reads=["t1"], writes=["t2"])
            op("scalar", lambda e: e.activation(out=t1[:], in_=PS1[:, 0:1024], func=AF.Relu),
               reads=["p1a", "p1b"], writes=["t1"])
            op("vector", lambda e: e.tensor_tensor(out=qhat[:], in0=t2[:], in1=t1[:], op=ALU.add),
               reads=["t2", "t1"], writes=["qhat"])
            kv_mms(0, x_fma, "x_fma")
            if c + 1 < n_own:
                xb2, xn2 = (xa, "xa") if (c + 1) % 2 == 0 else (xb, "xb")
                x_load_transpose(xo_h, c + 1, xb2, xn2, 1)
            elu_v(0)
            drain_b(b3)
            for d in range(8):
                op("tensor", lambda e, d=d: e.matmul(PS1[:, 1024 + 128 * d:1152 + 128 * d],
                                                     khat[:, 128 * d:128 * d + 128], ident[:],
                                                     start=True, stop=True),
                   reads=["khat", "ident"], writes=banksr(1, 1024 + 128 * d, 1152 + 128 * d))
            op("vector", lambda e: e.tensor_copy(
                out=k_fm.ap()[0:64, :].rearrange("p (cc n) -> p cc n", n=128)[:, 0:16:2, :],
                in_=PS1.ap()[0:64, 1024:2048].rearrange("p (cc n) -> p cc n", n=128)),
               reads=["p1c", "p1d"], writes=["k_fm"])
            op("vector", lambda e: e.tensor_copy(
                out=k_fm.ap()[64:128, :].rearrange("p (cc n) -> p cc n", n=128)[:, 1:16:2, :],
                in_=PS1.ap()[64:128, 1024:2048].rearrange("p (cc n) -> p cc n", n=128)),
               reads=["p1c", "p1d"], writes=["k_fm"])
            for s in range(2):
                bk = bank(1, s)
                op("tensor", lambda e, s=s: e.matmul(PS1[:, 512 * s:512 * s + 512], umask[:],
                                                     khat[:, 512 * s:512 * s + 512],
                                                     start=True, stop=False),
                   reads=["umask", "khat"], writes=[bk])
                op("tensor", lambda e, s=s: e.matmul(PS1[:, 512 * s:512 * s + 512], ones_fb[:],
                                                     ksr_b[:, 512 * s:512 * s + 512],
                                                     start=False, stop=True),
                   reads=["ones_fb", "ksr_b"], writes=[bk])
            drain_b(b3)
            op("vector", lambda e: e.tensor_tensor(out=t2[:], in0=PS1[:, 0:1024], in1=qhat[:], op=ALU.mult),
               reads=["p1a", "p1b", "qhat"], writes=["t2"])
            op("vector", lambda e: e.tensor_reduce(out=dr32[:, 0:16],
                                                   in_=t2.ap().rearrange("p (h d) -> p h d", d=DH),
                                                   axis=AX.X, op=ALU.add),
               reads=["t2"], writes=["dr32"])
            op("vector", lambda e: e.tensor_scalar_add(dr32[:, 16:32], dr32[:, 0:16], EPS_ATTN),
               reads=["dr32"], writes=["dr32"])
            op("vector", lambda e: e.reciprocal(out=dr32[:, 0:16], in_=dr32[:, 16:32]),
               reads=["dr32"], writes=["dr32"])
            op("vector", lambda e: e.tensor_tensor(
                out=qp_tm.ap().rearrange("p (h d) -> p h d", d=DH),
                in0=qhat.ap().rearrange("p (h d) -> p h d", d=DH),
                in1=dr32.ap()[:, 0:16].unsqueeze(2).to_broadcast([128, 16, DH]),
                op=ALU.mult),
               reads=["qhat", "dr32"], writes=["qp_tm"])
            for d in range(8):
                op("tensor", lambda e, d=d: e.matmul(PS1[:, 1024 + 128 * d:1152 + 128 * d],
                                                     qp_tm[:, 128 * d:128 * d + 128], ident[:],
                                                     start=True, stop=True),
                   reads=["qp_tm", "ident"], writes=banksr(1, 1024 + 128 * d, 1152 + 128 * d))
            op("scalar", lambda e: e.copy(out=qp_fm[:], in_=PS1[:, 1024:2048]),
               reads=["p1c", "p1d"], writes=["qp_fm"])
            for half in range(2):
                for h in range(8 * half, 8 * half + 8):
                    fo = 128 * (h // 2)
                    op("tensor", lambda e, h=h, fo=fo: e.matmul(
                        PS0[:, 128 * (h % 8):128 * (h % 8) + 128],
                        k_fm[:, 128 * h:128 * h + 128], qp_fm[:, fo:fo + 128],
                        start=True, stop=True),
                       reads=["k_fm", "qp_fm"], writes=banksr(0, 128 * (h % 8), 128 * (h % 8) + 128))
                op("vector", lambda e: e.tensor_tensor(
                    out=st_m.ap().rearrange("p (h t) -> p h t", t=128),
                    in0=PS0.ap()[:, 0:1024].rearrange("p (h t) -> p h t", t=128),
                    in1=umask.ap().unsqueeze(1).to_broadcast([128, 8, 128]),
                    op=ALU.mult),
                   reads=["p0a", "p0b", "umask"], writes=["st_m"])
                if half == 0:
                    drain_b(b3)
                for h in range(8 * half, 8 * half + 8):
                    pb, fo = 64 * (h % 2), 128 * (h // 2)
                    bk = "p1a" if h // 2 < 4 else "p1b"
                    op("tensor", lambda e, h=h, pb=pb, fo=fo: e.matmul(
                        PS1[pb:pb + 64, fo:fo + 128],
                        v_tm[:, 64 * h:64 * h + 64], st_m[:, 128 * (h % 8):128 * (h % 8) + 128],
                        start=True, stop=False),
                       reads=["v_tm", "st_m"], writes=[bk])
                    op("tensor", lambda e, h=h, pb=pb, fo=fo: e.matmul(
                        PS1[pb:pb + 64, fo:fo + 128],
                        state_b[:, 64 * h:64 * h + 64],
                        qp_fm[:, fo:fo + 128],
                        start=False, stop=True),
                       reads=["state_b", "qp_fm"], writes=[bk])
            op("scalar", lambda e: e.copy(out=attn[:], in_=PS1[:, 0:1024]),
               reads=["p1a", "p1b"], writes=["attn"])
            state_update(0, prefix=False)
            drain_b(b3)
            for d in range(8):
                for s in range(2):
                    op("tensor", lambda e, d=d, s=s: e.matmul(
                        PS1[:, 512 * s:512 * s + 512],
                        attn[:, 128 * d:128 * d + 128],
                        wo_s[:, 1024 * d + 512 * s: 1024 * d + 512 * s + 512],
                        start=(d == 0), stop=(d == 7)),
                       reads=["attn", f"wo{d}"], writes=[bank(1, s)])
            op("vector", lambda e, xbuf=xbuf: e.scalar_tensor_tensor(
                out=r_sb[:], in0=PS1[:, 0:1024], scalar=1.0, in1=xbuf[:],
                op0=ALU.mult, op1=ALU.add, accum_out=stx[:, 0:1]),
               reads=["p1a", "p1b", xname], writes=["r_sb", "stx"])
            op("vector", lambda e: e.scalar_tensor_tensor(
                out=t2[:], in0=r_sb[:], scalar=1.0, in1=r_sb[:],
                op0=ALU.mult, op1=ALU.mult, accum_out=stx[:, 1:2]),
               reads=["r_sb"], writes=["t2", "stx"])
            tq4 = c % 4
            x1p, x1pn = (x1_s, "x1_s") if (c // 4) % 2 == 0 else (x1_s2, "x1_s2")
            layer_norm_tail(r_sb[:], ["r_sb"], x1p[:, D * tq4:D * tq4 + D], x1pn,
                            stx[:, 0:1], stx[:, 1:2], stx[:, 2:3], stx[:, 3:4],
                            "stx", "stx", "stx", "stx",
                            acc_ap=bt1[:, tq4:tq4 + 1], acc_w=["bty"])
            op("vector", lambda e, tq4=tq4, x1p=x1p: e.scalar_tensor_tensor(
                out=u_s[:], in0=x1p[:, D * tq4:D * tq4 + D], scalar=1.0,
                in1=x1p[:, D * tq4:D * tq4 + D], op0=ALU.mult, op1=ALU.mult,
                accum_out=bt2[:, tq4:tq4 + 1]),
               reads=[x1pn], writes=["u_s", "bty"])
            drain_b(b3)
        if n_own == NCH and n_b == 8:
            drain_b(len(B_QUEUE))
            if PENDING[0] is not None:
                PENDING[0]()
                PENDING[0] = None
            emit_b_super(7)
        drain_b(len(B_QUEUE))
        if PENDING[0] is not None:
            PENDING[0]()
            PENDING[0] = None

        S.wait_all("sync", S.snapshot())

        with nc.Block() as blk:
            @blk.sync
            def _(eng):
                S.replay("sync", eng, sems)

            @blk.tensor
            def _(eng):
                S.replay("tensor", eng, sems)

            @blk.vector
            def _(eng):
                S.replay("vector", eng, sems)

            @blk.scalar
            def _(eng):
                S.replay("scalar", eng, sems)

            @blk.gpsimd
            def _(eng):
                S.replay("gpsimd", eng, sems)

    return nc


def kernel(L0_attn_x, L0_attn_Wq, L0_attn_Wk, L0_attn_Wv, L0_attn_Wo,
           L0_ffn_ln_gamma, L0_ffn_ln_beta, L0_ffn_fc1_q, L0_ffn_fc1_scales,
           L0_ffn_fc1_bias, L0_ffn_fc2_q, L0_ffn_fc2_scales, L0_ffn_fc2_bias):
    x = np.ascontiguousarray(np.asarray(L0_attn_x, dtype=np.float32))
    wqkvT = np.ascontiguousarray(
        np.concatenate([np.asarray(L0_attn_Wq).T, np.asarray(L0_attn_Wk).T,
                        np.asarray(L0_attn_Wv).T], axis=1).astype(np.float32))
    woT = np.ascontiguousarray(np.asarray(L0_attn_Wo).T.astype(np.float32))
    fc1q = np.ascontiguousarray(np.asarray(L0_ffn_fc1_q, dtype=np.int32))
    fc2q = np.ascontiguousarray(np.asarray(L0_ffn_fc2_q, dtype=np.int32))
    fc1s = np.ascontiguousarray(np.asarray(L0_ffn_fc1_scales, dtype=np.float32))
    fc2s = np.ascontiguousarray(np.asarray(L0_ffn_fc2_scales, dtype=np.float32))
    fc1sb = np.ascontiguousarray(np.broadcast_to(fc1s[:, None, :], (8, 128, F)))
    fc2sb = np.ascontiguousarray(np.broadcast_to(fc2s[:, None, :], (32, 128, D)))
    fc1b = np.ascontiguousarray(np.asarray(L0_ffn_fc1_bias, dtype=np.float32).reshape(32, 128).T)
    fc2b = np.ascontiguousarray(np.asarray(L0_ffn_fc2_bias, dtype=np.float32).reshape(1, D))
    gam = np.ascontiguousarray(np.asarray(L0_ffn_ln_gamma, dtype=np.float32).reshape(1, D))
    bet = np.ascontiguousarray(np.asarray(L0_ffn_ln_beta, dtype=np.float32).reshape(1, D))
    gamc = np.ascontiguousarray(np.asarray(L0_ffn_ln_gamma, dtype=np.float32).reshape(8, 128).T)
    betc = np.ascontiguousarray(np.asarray(L0_ffn_ln_beta, dtype=np.float32).reshape(8, 128).T)
    umaskI = np.triu(np.ones((128, 128), np.float32))
    identI = np.eye(128, dtype=np.float32)
    zeros_x = np.zeros((TOK, D), np.float32)

    nc = build_program()
    in_maps = []
    for c in range(8):
        b, j = c // 2, c % 2
        in_maps.append({
            "xo": np.ascontiguousarray(x[b, TOK * j:TOK * j + TOK]),
            "xp": np.ascontiguousarray(x[b, 0:TOK]) if j == 1 else zeros_x,
            "amask": np.array([[1.0 if j == 1 else 0.0]], np.float32),
            "wqkvT": wqkvT, "woT": woT,
            "fc1q": fc1q, "fc1s": fc1s, "fc1b": fc1b, "fc1sb": fc1sb, "fc2sb": fc2sb,
            "fc2q": fc2q, "fc2s": fc2s, "fc2b": fc2b,
            "gam": gam, "bet": bet, "gamC": gamc, "betC": betc,
            "umaskI": umaskI, "identI": identI,
        })
    res = run_bass_kernel_spmd(nc, in_maps, core_ids=list(range(8)))
    out = np.empty((4, 2 * TOK, D), np.float32)
    for c in range(8):
        b, j = c // 2, c % 2
        out[b, TOK * j:TOK * j + TOK] = np.asarray(res.results[c]["out"])
    if DEBUG:
        global DBG_RES
        DBG_RES = res.results
    return out


# version 27
# speedup vs baseline: 1.0303x; 1.0099x over previous
import numpy as np
import ml_dtypes
DBG_RES = None
import concourse.bass as bass
import concourse.mybir as mybir
import concourse.bass_utils as _bu
from concourse.bass_utils import run_bass_kernel_spmd
from contextlib import ExitStack

F32 = mybir.dt.float32
BF16 = mybir.dt.bfloat16
I32 = mybir.dt.int32
AF = mybir.ActivationFunctionType
ALU = mybir.AluOpType
AX = mybir.AxisListType
BF = ml_dtypes.bfloat16

_orig_bvo = _bu.bir_verify_and_optimise
def _bvo_fast(tmpdir, inp="bir.json", outp="file.neff", arch=None, *, dve_root=None):
    real_run = _bu.run_command
    def run_patched(cmd, cwd=None, **kw):
        cmd = ["--enable-birsim=false" if c == "--enable-birsim=true" else c for c in cmd]
        return real_run(cmd, cwd=cwd, **kw)
    _bu.run_command = run_patched
    try:
        return _orig_bvo(tmpdir, inp, outp, arch, dve_root=dve_root)
    finally:
        _bu.run_command = real_run
_bu.bir_verify_and_optimise = _bvo_fast

D = 1024
H = 16
DH = 64
TOK = 4096
NCH = 32
F = 4096
EPS_ATTN = 1e-8
EPS_LN = 1e-5

ENG_SEM = {"tensor": "pe", "vector": "dv", "scalar": "ac", "gpsimd": "po"}
DMA_SEM = {"sync": "sd", "gpsimd": "gd"}
DTSIZE = {F32: 4, BF16: 2, I32: 4}


class Sched:

    NLANES = 8

    def __init__(self):
        self.ops = {k: [] for k in ("sync", "tensor", "vector", "scalar", "gpsimd")}
        self.counts = {}
        for s in ("pe", "dv", "ac", "po"):
            self.counts[s] = 0
        for base in ("sd", "gd", "sr"):
            for l in range(self.NLANES):
                self.counts[base + str(l)] = 0
        self.lane = {"sd": 0, "gd": 0, "sr": 0}
        self.bufs = {}
        self.floor = {}

    def _mark(self, name):
        if name not in self.bufs:
            self.bufs[name] = {"w": None, "r": []}
        return self.bufs[name]

    def op(self, engine, build, reads=(), writes=(), dma=False, sem=None, label=""):
        if dma:
            base = sem or DMA_SEM[engine]
            sem = base + str(self.lane[base])
            self.lane[base] = (self.lane[base] + 1) % self.NLANES
            inc = 16
        else:
            sem = ENG_SEM[engine]
            inc = 1
        waits = {}
        def add_wait(mark):
            if mark is None:
                return
            s, v = mark
            if not dma and s == sem and engine == "tensor":
                return
            if waits.get(s, 0) < v:
                waits[s] = v
        for b in reads:
            add_wait(self._mark(b)["w"])
        for b in writes:
            m = self._mark(b)
            add_wait(m["w"])
            for rm in m["r"]:
                add_wait(rm)
        fl = []
        for s, v in waits.items():
            key = (engine, s)
            if self.floor.get(key, 0) < v:
                fl.append((s, v))
                self.floor[key] = v
        self.counts[sem] += inc
        mark = (sem, self.counts[sem])
        for b in reads:
            self._mark(b)["r"].append(mark)
        for b in writes:
            m = self._mark(b)
            m["w"] = mark
            m["r"] = []
        self.ops[engine].append((tuple(fl), build, sem, inc, label))
        return mark

    def wait_all(self, engine, marks):
        fl = []
        for s, v in marks:
            if s == ENG_SEM.get(engine):
                continue
            key = (engine, s)
            if self.floor.get(key, 0) < v:
                fl.append((s, v))
                self.floor[key] = v
        if fl:
            self.ops[engine].append((tuple(fl), None, None, 0, "barrier"))

    def snapshot(self):
        return [(s, v) for s, v in self.counts.items() if v > 0]

    def replay(self, engine, eng, sems):
        for fl, build, sem, inc, label in self.ops[engine]:
            for s, v in fl:
                eng.wait_ge(sems[s], v)
            if build is not None:
                build(eng).then_inc(sems[sem], inc)


STAGES = [NCH, NCH, 8]
DEBUG = False

def build_program():
    n_pref, n_own, n_b = STAGES
    nc = bass.Bass("TRN2", target_bir_lowering=False)

    xo_h = nc.dram_tensor("xo", [TOK, D], F32, kind="ExternalInput")
    xp_h = nc.dram_tensor("xp", [TOK, D], F32, kind="ExternalInput")
    amask_h = nc.dram_tensor("amask", [1, 1], F32, kind="ExternalInput")
    wqkv_h = nc.dram_tensor("wqkvT", [D, 3 * D], F32, kind="ExternalInput")
    wo_h = nc.dram_tensor("woT", [D, D], F32, kind="ExternalInput")
    fc1q_h = nc.dram_tensor("fc1q", [D, F], I32, kind="ExternalInput")
    fc1s_h = nc.dram_tensor("fc1s", [8, F], F32, kind="ExternalInput")
    fc1sb_h = nc.dram_tensor("fc1sb", [8, 128, F], F32, kind="ExternalInput")
    fc2sb_h = nc.dram_tensor("fc2sb", [32, 128, D], F32, kind="ExternalInput")
    fc1b_h = nc.dram_tensor("fc1b", [128, 32], F32, kind="ExternalInput")
    fc2q_h = nc.dram_tensor("fc2q", [F, D], I32, kind="ExternalInput")
    fc2s_h = nc.dram_tensor("fc2s", [32, D], F32, kind="ExternalInput")
    fc2b_h = nc.dram_tensor("fc2b", [1, D], F32, kind="ExternalInput")
    gam_h = nc.dram_tensor("gam", [1, D], F32, kind="ExternalInput")
    bet_h = nc.dram_tensor("bet", [1, D], F32, kind="ExternalInput")
    gamc_h = nc.dram_tensor("gamC", [128, 8], F32, kind="ExternalInput")
    betc_h = nc.dram_tensor("betC", [128, 8], F32, kind="ExternalInput")
    umask_h = nc.dram_tensor("umaskI", [128, 128], F32, kind="ExternalInput")
    ident_h = nc.dram_tensor("identI", [128, 128], F32, kind="ExternalInput")
    out_h = nc.dram_tensor("out", [TOK, D], F32, kind="ExternalOutput")
    dbg = "ExternalOutput" if DEBUG else "Internal"
    x1_d = nc.dram_tensor("x1d", [TOK, D], BF16, kind=dbg)
    w1_d = nc.dram_tensor("w1d", [256 * 128, 128], BF16, kind=dbg)
    w2_d = nc.dram_tensor("w2d", [F, D], BF16, kind=dbg)

    S = Sched()
    op = S.op

    with ExitStack() as ctx:
        def pers(name, shape, dtype):
            return ctx.enter_context(nc.sbuf_tensor(name, shape, dtype))

        state_f = pers("state_f", [128, 512], F32)
        state_b = pers("state_b", [128, 1024], BF16)
        ksr_f = pers("ksr_f", [1, D], F32)
        ksr_b = pers("ksr_b", [128, D], BF16)
        umask = pers("umask", [128, 128], BF16)
        ident = pers("ident", [128, 128], BF16)
        ones_fb = pers("ones_fb", [128, 128], BF16)
        ones_ff = pers("ones_ff", [128, 128], F32)
        ones_c = pers("ones_c", [128, 1], BF16)
        amask_s = pers("amask_s", [1, 1], F32)
        b1_s = pers("b1_s", [128, 32], F32)
        gamC = pers("gamC_s", [128, 8], F32)
        betC = pers("betC_s", [128, 8], F32)
        stx = pers("stx", [128, 4], F32)
        bty = pers("bty", [128, 16], F32)
        bt1 = bty.ap()[:, 0:4]
        bt2 = bty.ap()[:, 4:8]
        bt4 = bty.ap()[:, 8:12]
        bt5 = bty.ap()[:, 12:16]
        btx = pers("btx", [128, 16], F32)

        arena_base = (nc.sbuf_base + 63) & ~63
        arena_top = nc.sbuf_top

        def make_arena(base):
            cur = [base]
            def alloc(name, shape, dtype):
                nbytes = shape[1] * DTSIZE[dtype]
                off = cur[0]
                cur[0] = (off + nbytes + 63) & ~63
                assert cur[0] <= arena_top, f"sbuf arena overflow at {name}: {cur[0]} > {arena_top}"
                return nc.alloc_sbuf_tensor_at(name, shape, dtype, offset=off)
            return alloc, cur

        A, acur = make_arena(arena_base)
        wqkv = A("wqkv", [128, 8 * 3 * D], BF16)
        wo_s = A("wo_s", [128, 8 * D], BF16)
        xa = A("xa", [128, D], BF16)
        xb = A("xb", [128, D], BF16)
        x_fma = A("x_fma", [128, D], BF16)
        x_fmb = A("x_fmb", [128, D], BF16)
        t1 = A("t1", [128, D], BF16)
        t2 = A("t2", [128, D], BF16)
        qhat = A("qhat", [128, D], BF16)
        khat = A("khat", [128, D], BF16)
        v_tm = A("v_tm", [128, D], BF16)
        k_fm = A("k_fm", [128, 2 * D], BF16)
        qp_tm = A("qp_tm", [128, D], BF16)
        qp_fm = A("qp_fm", [128, D], BF16)
        st_m = A("st_m", [128, 1024], BF16)
        attn = A("attn", [128, D], BF16)
        r_sb = A("r_sb", [128, D], F32)
        dr32 = A("dr32", [128, 32], F32)
        g1base = acur[0]
        G, gcur = make_arena(g1base)
        g1a = G("g1a", [128, F], F32)
        g1b = G("g1b", [128, F], F32)
        g2a = G("g2a", [128, D], F32)
        g2b = G("g2b", [128, D], F32)
        wsl_a = G("wsl_a", [128, 512], BF16)
        wsl_b = G("wsl_b", [128, 512], BF16)
        wsl_c = G("wsl_c", [128, 512], BF16)
        wsl_d = G("wsl_d", [128, 512], BF16)
        s1b_a = G("s1b_a", [128, F], F32)
        s1b_b = G("s1b_b", [128, F], F32)
        s2b_a = G("s2b_a", [128, D], F32)
        s2b_b = G("s2b_b", [128, D], F32)
        B, bcur = make_arena(g1base)
        w1r_a = B("w1r_a", [128, 4096], BF16)
        w1r_b = B("w1r_b", [128, 4096], BF16)
        w2r_a = B("w2r_a", [128, 4096], BF16)
        w2r_b = B("w2r_b", [128, 4096], BF16)
        y_fm = B("y_fm", [128, 8 * 512], BF16)
        x1_s = B("x1_s", [128, 4 * D], BF16)
        x1_s2 = B("x1_s2", [128, 4 * D], BF16)
        h_fm = B("h_fm", [128, 4 * D], BF16)
        u_bf_a = B("u_bf_a", [128, D], BF16)
        u_bf_b = B("u_bf_b", [128, D], BF16)
        b2h_b = B("b2h_b", [128, D], BF16)
        u_s = B("u_s", [128, D], F32)
        r2_s = B("r2_s", [128, D], F32)
        z_sb = B("z_sb", [128, 4 * D], F32)
        out_s = B("out_s", [128, D], F32)

        PS0 = ctx.enter_context(nc.psum_tensor("PS0", [128, 2048], F32))
        PS1 = ctx.enter_context(nc.psum_tensor("PS1", [128, 2048], F32))
        PS = [PS0, PS1]
        def bank(g, k):
            return f"p{g}{'abcd'[k]}"
        def banksr(g, lo, hi):
            return [bank(g, k) for k in range(lo // 512, (hi + 511) // 512)]

        semnames = ["pe", "dv", "ac", "po"] + [b + str(l) for b in ("sd", "gd", "sr")
                                               for l in range(Sched.NLANES)]
        sems = {n: ctx.enter_context(nc.semaphore(n)) for n in semnames}

        op("vector", lambda e: e.memset(state_f[:], 0.0), writes=["state_f"])
        op("vector", lambda e: e.memset(state_b[:], 0.0), writes=["state_b"])
        op("vector", lambda e: e.memset(ksr_f[:], 0.0), writes=["ksr_f"])
        op("vector", lambda e: e.memset(ksr_b[:], 0.0), writes=["ksr_b"])
        op("vector", lambda e: e.memset(ones_fb[:], 1.0), writes=["ones_fb"])
        op("vector", lambda e: e.memset(ones_ff[:], 1.0), writes=["ones_ff"])
        op("vector", lambda e: e.memset(ones_c[:], 1.0), writes=["ones_c"])
        op("vector", lambda e: e.memset(k_fm[:], 0.0), writes=["k_fm"])
        op("sync", lambda e: e.dma_start(out=amask_s[:], in_=amask_h.ap()), dma=True, sem="sr", writes=["amask_s"])
        op("sync", lambda e: e.dma_start(out=b1_s[:], in_=fc1b_h.ap()), dma=True, writes=["b1_s"])
        op("sync", lambda e: e.dma_start(out=gamC[:], in_=gamc_h.ap()), dma=True, writes=["gamC"])
        op("sync", lambda e: e.dma_start(out=betC[:], in_=betc_h.ap()), dma=True, writes=["betC"])
        op("gpsimd", lambda e: e.dma_start(out=umask[:], in_=umask_h.ap()), dma=True, writes=["umask"])
        op("gpsimd", lambda e: e.dma_start(out=ident[:], in_=ident_h.ap()), dma=True, writes=["ident"])
        wq_in = wqkv_h.ap().rearrange("(c p) n -> p c n", p=128)
        for dd in range(8):
            op("gpsimd", lambda e, dd=dd: e.dma_start(out=wqkv[:, 3072 * dd:3072 * dd + 3072],
                                                      in_=wq_in[:, dd, :]),
               dma=True, writes=[f"wqkv{dd}"])
        wo_in = wo_h.ap().rearrange("(c p) n -> p c n", p=128)
        for dd in range(8):
            op("gpsimd", lambda e, dd=dd: e.dma_start(out=wo_s[:, 1024 * dd:1024 * dd + 1024],
                                                      in_=wo_in[:, dd, :]),
               dma=True, writes=[f"wo{dd}"])

        def x_load_transpose(src_h, c, xbuf, xname, tg):
            op("gpsimd", lambda e: e.dma_start(out=xbuf[:], in_=src_h.ap()[128 * c:128 * c + 128, :]),
               dma=True, writes=[xname])
            for d in range(8):
                op("tensor", lambda e, d=d: e.matmul(PS[tg][:, 1024 + 128 * d:1152 + 128 * d],
                                                     xbuf[:, 128 * d:128 * d + 128], ident[:],
                                                     start=True, stop=True),
                   reads=[xname, "ident"], writes=banksr(tg, 1024 + 128 * d, 1152 + 128 * d))
            op("scalar", lambda e: e.copy(out=x_fma[:], in_=PS[tg][:, 1024:2048]),
               reads=[bank(tg, 2), bank(tg, 3)], writes=["x_fma"])

        def x_load_pref(src_h, c):
            xbuf, xname = (xa, "xa") if c % 2 == 0 else (xb, "xb")
            op("gpsimd", lambda e, xbuf=xbuf, c=c: e.dma_start(
                out=xbuf[:], in_=src_h.ap()[128 * c:128 * c + 128, :]),
               dma=True, writes=[xname])

        def x_transpose_pipe(c):
            j = c % 2
            xbuf, xname = (xa, "xa") if c % 2 == 0 else (xb, "xb")
            xfm, xfmn = (x_fma, "x_fma") if j == 0 else (x_fmb, "x_fmb")
            for w in range(2):
                for d in range(4):
                    dd = 4 * w + d
                    op("tensor", lambda e, dd=dd, d=d, xbuf=xbuf: e.matmul(
                        PS[j][:, 1536 + 128 * d:1664 + 128 * d],
                        xbuf[:, 128 * dd:128 * dd + 128], ident[:],
                        start=True, stop=True),
                       reads=[xname, "ident"], writes=[bank(j, 3)])
                op("vector", lambda e, w=w, xfm=xfm: e.tensor_copy(
                    out=xfm[:, 512 * w:512 * w + 512], in_=PS[j][:, 1536:2048]),
                   reads=[bank(j, 3)], writes=[xfmn])

        def kv_mms(g, xfm, xfmn):
            for d in range(8):
                for s in range(4):
                    op("tensor", lambda e, d=d, s=s, xfm=xfm: e.matmul(
                        PS[g][:, 512 * s:512 * s + 512],
                        xfm[:, 128 * d:128 * d + 128],
                        wqkv[:, 3072 * d + D + 512 * s: 3072 * d + D + 512 * s + 512],
                        start=(d == 0), stop=(d == 7)),
                       reads=[xfmn, f"wqkv{d}"], writes=[bank(g, s)])

        def elu_acts(g):
            op("scalar", lambda e: e.activation(out=t1[:], in_=PS[g][:, 0:1024], func=AF.Relu, scale=-1.0),
               reads=[bank(g, 0), bank(g, 1)], writes=["t1"])
            op("scalar", lambda e: e.activation(out=t2[:], in_=t1[:], func=AF.Exp, scale=-1.0),
               reads=["t1"], writes=["t2"])
            op("scalar", lambda e: e.activation(out=t1[:], in_=PS[g][:, 0:1024], func=AF.Relu),
               reads=[bank(g, 0), bank(g, 1)], writes=["t1"])

        def khat_add():
            op("vector", lambda e: e.tensor_tensor(out=khat[:], in0=t2[:], in1=t1[:], op=ALU.add),
               reads=["t2", "t1"], writes=["khat"])

        def v_evac(g):
            op("vector", lambda e: e.tensor_copy(out=v_tm[:], in_=PS[g][:, 1024:2048]),
               reads=[bank(g, 2), bank(g, 3)], writes=["v_tm"])

        def elu_v(g):
            elu_acts(g)
            khat_add()
            v_evac(g)

        def state_update(g, prefix):
            for h in range(16):
                op("tensor", lambda e, h=h: e.matmul(
                    PS[g][64 * (h % 2):64 * (h % 2) + 64, 64 * (h // 2):64 * (h // 2) + 64],
                    khat[:, 64 * h:64 * h + 64], v_tm[:, 64 * h:64 * h + 64],
                    start=True, stop=True),
                   reads=["khat", "v_tm"], writes=[bank(g, 0)])
            op("vector", lambda e: e.tensor_tensor(out=state_f[:], in0=PS[g][:, 0:512], in1=state_f[:], op=ALU.add),
               reads=[bank(g, 0), "state_f"], writes=["state_f"])
            if not prefix:
                op("vector", lambda e: e.tensor_copy(
                    out=state_b.ap()[0:64, :].rearrange("p (c n) -> p c n", n=64)[:, 0:16:2, :],
                    in_=state_f.ap()[0:64, :].rearrange("p (c n) -> p c n", n=64)),
                   reads=["state_f"], writes=["state_b"])
                op("vector", lambda e: e.tensor_copy(
                    out=state_b.ap()[64:128, :].rearrange("p (c n) -> p c n", n=64)[:, 1:16:2, :],
                    in_=state_f.ap()[64:128, :].rearrange("p (c n) -> p c n", n=64)),
                   reads=["state_f"], writes=["state_b"])
            for s in range(2):
                bk = bank(g, 1 + s)
                lo = 512 + 512 * s
                op("tensor", lambda e, s=s, lo=lo: e.matmul(
                    PS[g][0:1, lo:lo + 512], ones_c[:], khat[:, 512 * s:512 * s + 512],
                    start=True, stop=True),
                   reads=["ones_c", "khat"], writes=[bk])
                if prefix:
                    op("vector", lambda e, s=s, lo=lo: e.scalar_tensor_tensor(
                        out=ksr_f[0:1, 512 * s:512 * s + 512], in0=PS[g][0:1, lo:lo + 512],
                        scalar=amask_s[0:1, :], in1=ksr_f[0:1, 512 * s:512 * s + 512],
                        op0=ALU.mult, op1=ALU.add),
                       reads=[bk, "ksr_f", "amask_s"], writes=["ksr_f"])
                else:
                    op("vector", lambda e, s=s, lo=lo: e.tensor_tensor(
                        out=ksr_f[0:1, 512 * s:512 * s + 512], in0=PS[g][0:1, lo:lo + 512],
                        in1=ksr_f[0:1, 512 * s:512 * s + 512], op=ALU.add),
                       reads=[bk, "ksr_f"], writes=["ksr_f"])
            if not prefix:
                op("vector", lambda e: e.tensor_copy(out=ksr_b[0:1, :], in_=ksr_f[:]),
                   reads=["ksr_f"], writes=["ksr_b"])

        def dequant_slice(qgran, gname, scltile, sclname, outap, wd_name, wsl, wslname, scol):
            op("vector", lambda e: e.scalar_tensor_tensor(
                out=wsl[:], in0=qgran[:, scol:scol + 512], scalar=-128.0,
                in1=scltile[:, scol:scol + 512], op0=ALU.add, op1=ALU.mult),
               reads=[gname, sclname], writes=[wslname])
            op("sync", lambda e: e.dma_start(out=outap, in_=wsl[:]),
               dma=True, reads=[wslname], writes=[wd_name])

        def layer_norm_tail(src, src_reads, outap, outname, s1, s2, s4, s5, n1, n2, n4, n5,
                            acc_ap=None, acc_w=[]):
            op("vector", lambda e: e.tensor_scalar_mul(s1, s1, 1.0 / D),
               reads=[n1], writes=[n1])
            op("vector", lambda e: e.tensor_tensor(out=s5, in0=s1, in1=s1, op=ALU.mult),
               reads=[n1], writes=[n5])
            op("vector", lambda e: e.scalar_tensor_tensor(
                out=s2, in0=s2, scalar=1.0 / D, in1=s5, op0=ALU.mult, op1=ALU.subtract),
               reads=[n2, n5], writes=[n2])
            op("vector", lambda e: e.tensor_scalar_add(s2, s2, EPS_LN),
               reads=[n2], writes=[n2])
            op("scalar", lambda e: e.activation(out=s4, in_=s2, func=AF.Ln),
               reads=[n2], writes=[n4])
            op("scalar", lambda e: e.activation(out=s4, in_=s4, func=AF.Exp, scale=-0.5),
               reads=[n4], writes=[n4])
            op("vector", lambda e: e.scalar_tensor_tensor(
                out=s5, in0=s1, scalar=-1.0, in1=s4, op0=ALU.mult, op1=ALU.mult),
               reads=[n1, n4], writes=[n5])
            op("vector", lambda e: e.scalar_tensor_tensor(
                out=outap, in0=src, scalar=s4, in1=s5.to_broadcast([128, D]),
                op0=ALU.mult, op1=ALU.add, accum_out=acc_ap),
               reads=list(src_reads) + [n4, n5], writes=[outname] + acc_w)

        B_QUEUE = []

        def bq(fn):
            B_QUEUE.append(fn)

        def ln_rstd(b1, b2, b4, b5, n1, n2, n4, n5):
            op("vector", lambda e: e.tensor_scalar_mul(b1, b1, 1.0 / D),
               reads=[n1], writes=[n1])
            op("vector", lambda e: e.tensor_tensor(out=b5, in0=b1, in1=b1, op=ALU.mult),
               reads=[n1], writes=[n5])
            op("vector", lambda e: e.scalar_tensor_tensor(
                out=b2, in0=b2, scalar=1.0 / D, in1=b5, op0=ALU.mult, op1=ALU.subtract),
               reads=[n2, n5], writes=[n2])
            op("vector", lambda e: e.tensor_scalar_add(b2, b2, EPS_LN),
               reads=[n2], writes=[n2])
            op("scalar", lambda e: e.activation(out=b4, in_=b2, func=AF.Ln),
               reads=[n2], writes=[n4])
            op("scalar", lambda e: e.activation(out=b4, in_=b4, func=AF.Exp, scale=-0.5),
               reads=[n4], writes=[n4])
            op("vector", lambda e: e.scalar_tensor_tensor(
                out=b5, in0=b1, scalar=-1.0, in1=b4, op0=ALU.mult, op1=ALU.mult),
               reads=[n1, n4], writes=[n5])

        PENDING = [None]

        def emit_b_super(sc):
            x1p, x1pn = (x1_s, "x1_s") if sc % 2 == 0 else (x1_s2, "x1_s2")
            ln_rstd(bt1, bt2, bt4, bt5, "bty", "bty", "bty", "bty")
            def apply_prep(t):
                ub, ubn = (u_bf_a, "u_bf_a") if t % 2 == 0 else (u_bf_b, "u_bf_b")
                def f(t=t, ub=ub, ubn=ubn):
                    op("scalar", lambda e, t=t, ub=ub: e.activation(
                        out=ub[:], in_=x1p[:, D * t:D * t + D], func=AF.Identity,
                        bias=bt5[:, t:t + 1], scale=bt4[:, t:t + 1]),
                       reads=[x1pn, "bty"], writes=[ubn])
                return f

            def apply_pe(t):
                ub, ubn = (u_bf_a, "u_bf_a") if t % 2 == 0 else (u_bf_b, "u_bf_b")
                def f(t=t, ub=ub, ubn=ubn):
                    for d in range(8):
                        op("tensor", lambda e, d=d, ub=ub: e.matmul(
                            PS1[:, 1024 + 128 * d:1152 + 128 * d],
                            ub[:, 128 * d:128 * d + 128], ident[:],
                            start=True, stop=True),
                           reads=[ubn, "ident"], writes=banksr(1, 1024 + 128 * d, 1152 + 128 * d))
                    for d in range(8):
                        op("scalar", lambda e, d=d, t=t: e.activation(
                            out=h_fm[:, 512 * d + 128 * t:512 * d + 128 * t + 128],
                            in_=PS1[:, 1024 + 128 * d:1152 + 128 * d], func=AF.Identity,
                            bias=betC[:, d:d + 1], scale=gamC[:, d:d + 1]),
                           reads=[bank(1, 2 + d // 4), "gamC", "betC"], writes=["h_fm"])
                return f
            bq(apply_prep(0))
            bq(apply_prep(1))
            bq(apply_pe(0))
            bq(apply_prep(2))
            bq(apply_pe(1))
            bq(apply_prep(3))
            bq(apply_pe(2))
            bq(apply_pe(3))
            def w1_load(mg):
                w1r, w1rn = (w1r_a, "w1r_a") if mg % 2 == 0 else (w1r_b, "w1r_b")
                def f(mg=mg, w1r=w1r, w1rn=w1rn):
                    op("sync", lambda e: e.dma_start(
                        out=w1r.ap().rearrange("p (t n) -> p t n", t=32),
                        in_=w1_d.ap().rearrange("(t p) n -> p t n", p=128)[:, 32 * mg:32 * mg + 32, :]),
                       dma=True, reads=["w1d"], writes=[w1rn])
                return f
            bq(w1_load(0))
            bq(w1_load(1))
            for q in range(4):
                def w2_loads(q=q):
                    for j, (buf, bn) in enumerate(((w2r_a, "w2r_a"), (w2r_b, "w2r_b"))):
                        e8 = 2 * q + j
                        op("sync", lambda e, buf=buf, e8=e8: e.dma_start(
                            out=buf.ap().rearrange("p (m n) -> p m n", m=4),
                            in_=w2_d.ap().rearrange("(m p) n -> p m n", p=128)[:, 4 * e8:4 * e8 + 4, :]),
                           dma=True, reads=["w2d"], writes=[bn])
                bq(w2_loads)
                for gi in range(2):
                    mg = 2 * q + gi
                    w1r, w1rn = (w1r_a, "w1r_a") if mg % 2 == 0 else (w1r_b, "w1r_b")
                    def fc1_item(mg=mg, w1r=w1r, w1rn=w1rn):
                        for mm in range(4):
                            m = 4 * mg + mm
                            yb = bank(0, m % 2)
                            lo = 512 * (m % 2)
                            for d in range(8):
                                op("tensor", lambda e, mm=mm, d=d, lo=lo, w1r=w1r: e.matmul(
                                    PS0[:, lo:lo + 512],
                                    w1r[:, 128 * (8 * mm + d):128 * (8 * mm + d) + 128],
                                    h_fm[:, 512 * d:512 * d + 512],
                                    start=(d == 0), stop=(d == 7)),
                                   reads=[w1rn, "h_fm"], writes=[yb])
                            op("scalar", lambda e, m=m, lo=lo: e.activation(
                                out=y_fm[:, 512 * (m % 8):512 * (m % 8) + 512], in_=PS0[:, lo:lo + 512],
                                func=AF.Silu, bias=b1_s[:, m:m + 1]),
                               reads=[yb, "b1_s"], writes=["y_fm"])
                    bq(fc1_item)
                    if mg + 2 < 8:
                        bq(w1_load(mg + 2))
                for t in range(4):
                    def fc2_item(q=q, t=t, sc=sc):
                        for m in range(8 * q, 8 * q + 8):
                            rbuf, rname = (w2r_a, "w2r_a") if (m // 4) % 2 == 0 else (w2r_b, "w2r_b")
                            for s in range(2):
                                op("tensor", lambda e, m=m, s=s, rbuf=rbuf: e.matmul(
                                    PS0[:, 1024 + 512 * s:1536 + 512 * s],
                                    y_fm[:, 512 * (m % 8) + 128 * t: 512 * (m % 8) + 128 * t + 128],
                                    rbuf[:, 1024 * (m % 4) + 512 * s: 1024 * (m % 4) + 512 * s + 512],
                                    start=(m % 8 == 0), stop=(m % 8 == 7)),
                                   reads=["y_fm", rname], writes=[bank(0, 2 + s)])
                        if q == 0:
                            op("vector", lambda e, t=t: e.tensor_copy(
                                out=z_sb[:, D * t:D * t + D], in_=PS0[:, 1024:2048]),
                               reads=["p0c", "p0d"], writes=["z_sb"])
                        else:
                            op("vector", lambda e, t=t: e.tensor_tensor(
                                out=z_sb[:, D * t:D * t + D], in0=PS0[:, 1024:2048],
                                in1=z_sb[:, D * t:D * t + D], op=ALU.add),
                               reads=["p0c", "p0d", "z_sb"], writes=["z_sb"])
                        if q == 3:
                            op("vector", lambda e, t=t: e.scalar_tensor_tensor(
                                out=u_s[:], in0=z_sb[:, D * t:D * t + D], scalar=0.5, in1=b2h_b[:],
                                op0=ALU.mult, op1=ALU.add),
                               reads=["z_sb", "b2h_b"], writes=["u_s"])
                            op("vector", lambda e, t=t: e.scalar_tensor_tensor(
                                out=r2_s[:], in0=u_s[:], scalar=1.0, in1=x1p[:, D * t:D * t + D],
                                op0=ALU.mult, op1=ALU.add, accum_out=btx[:, 8 + t:9 + t]),
                               reads=["u_s", x1pn], writes=["r2_s", "btx"])
                            op("vector", lambda e: e.scalar_tensor_tensor(
                                out=u_s[:], in0=r2_s[:], scalar=1.0, in1=r2_s[:],
                                op0=ALU.mult, op1=ALU.mult, accum_out=btx[:, 12 + t:13 + t]),
                               reads=["r2_s"], writes=["u_s", "btx"])
                    bq(fc2_item)

            def final_block(sc=sc, x1p=x1p, x1pn=x1pn):
                ln_rstd(btx[:, 8:12], btx[:, 12:16], btx[:, 0:4], btx[:, 4:8],
                        "btx", "btx", "btx", "btx")
                for t in range(4):
                    op("vector", lambda e, t=t: e.scalar_tensor_tensor(
                        out=u_s[:], in0=z_sb[:, D * t:D * t + D], scalar=0.5, in1=b2h_b[:],
                        op0=ALU.mult, op1=ALU.add),
                       reads=["z_sb", "b2h_b"], writes=["u_s"])
                    op("vector", lambda e, t=t, x1p=x1p: e.scalar_tensor_tensor(
                        out=r2_s[:], in0=u_s[:], scalar=1.0, in1=x1p[:, D * t:D * t + D],
                        op0=ALU.mult, op1=ALU.add),
                       reads=["u_s", x1pn], writes=["r2_s"])
                    op("scalar", lambda e, t=t: e.activation(
                        out=out_s[:], in_=r2_s[:], func=AF.Identity,
                        bias=btx[:, 4 + t:5 + t], scale=btx[:, t:t + 1]),
                       reads=["r2_s", "btx"], writes=["out_s"])
                    op("sync", lambda e, sc=sc, t=t: e.dma_start(
                        out=out_h.ap()[512 * sc + 128 * t: 512 * sc + 128 * t + 128, :], in_=out_s[:]),
                       dma=True, reads=["out_s"], writes=["out"])
            PENDING[0] = final_block

        def drain_b(n):
            for _ in range(min(n, len(B_QUEUE))):
                B_QUEUE.pop(0)()

        def deq_src_issue(cc):
            gq = cc // 4
            if cc % 4 == 0:
                gr, gn = (g1a, "g1a") if gq % 2 == 0 else (g1b, "g1b")
                s1, s1n = (s1b_a, "s1b_a") if gq % 2 == 0 else (s1b_b, "s1b_b")
                op("gpsimd", lambda e, gr=gr, gq=gq: e.dma_start(
                    out=gr[:], in_=fc1q_h.ap()[128 * gq:128 * gq + 128, :]),
                   dma=True, writes=[gn])
                op("sync", lambda e, s1=s1, gq=gq: e.dma_start(out=s1[:], in_=fc1sb_h.ap()[gq, :, :]),
                   dma=True, writes=[s1n])
            g2r, g2n = (g2a, "g2a") if cc % 2 == 0 else (g2b, "g2b")
            s2, s2n = (s2b_a, "s2b_a") if cc % 2 == 0 else (s2b_b, "s2b_b")
            op("gpsimd", lambda e, g2r=g2r, cc=cc: e.dma_start(
                out=g2r[:], in_=fc2q_h.ap()[128 * cc:128 * cc + 128, :]),
               dma=True, writes=[g2n])
            op("sync", lambda e, s2=s2, cc=cc: e.dma_start(out=s2[:], in_=fc2sb_h.ap()[cc, :, :]),
               dma=True, writes=[s2n])

        if n_pref > 0:
            x_load_pref(xp_h, 0)
            x_transpose_pipe(0)
            deq_src_issue(0)
        for c in range(n_pref):
            g = c % 2
            gq = c // 4
            if c + 1 < n_pref:
                x_load_pref(xp_h, c + 1)
                deq_src_issue(c + 1)
            kv_mms(g, *((x_fma, "x_fma") if g == 0 else (x_fmb, "x_fmb")))
            if c + 1 < n_pref:
                x_transpose_pipe(c + 1)
            if c > 0:
                state_update(1 - g, prefix=True)
            elu_acts(g)
            v_evac(g)
            khat_add()
            gr, gn = (g1a, "g1a") if gq % 2 == 0 else (g1b, "g1b")
            s1, s1n = (s1b_a, "s1b_a") if gq % 2 == 0 else (s1b_b, "s1b_b")
            g2r, g2n = (g2a, "g2a") if c % 2 == 0 else (g2b, "g2b")
            s2, s2n = (s2b_a, "s2b_a") if c % 2 == 0 else (s2b_b, "s2b_b")
            for k in range(2):
                sl = (c % 4) * 2 + k
                wsl, wsln = (wsl_a, "wsl_a") if k == 0 else (wsl_b, "wsl_b")
                outap = w1_d.ap().rearrange("(t p) n -> p t n", p=128)[
                    :, 32 * sl + gq: 32 * sl + gq + 25: 8, :]
                op("vector", lambda e, sl=sl, wsl=wsl, gr=gr, s1=s1: e.scalar_tensor_tensor(
                    out=wsl[:], in0=gr[:, 512 * sl:512 * sl + 512], scalar=-128.0,
                    in1=s1[:, 512 * sl:512 * sl + 512], op0=ALU.add, op1=ALU.mult),
                   reads=[gn, s1n], writes=[wsln])
                op("sync", lambda e, outap=outap, wsl=wsl: e.dma_start(
                    out=outap, in_=wsl.ap().rearrange("p (j n) -> p j n", j=4)),
                   dma=True, reads=[wsln], writes=["w1d"])
            for k in range(2):
                wsl, wsln = (wsl_c, "wsl_c") if k == 0 else (wsl_d, "wsl_d")
                dequant_slice(g2r, g2n, s2, s2n,
                              w2_d.ap()[128 * c:128 * c + 128, 512 * k:512 * k + 512],
                              "w2d", wsl, wsln, 512 * k)
        if n_pref > 0:
            state_update((n_pref - 1) % 2, prefix=True)
            op("vector", lambda e: e.tensor_copy(
                out=state_b.ap()[0:64, :].rearrange("p (c n) -> p c n", n=64)[:, 0:16:2, :],
                in_=state_f.ap()[0:64, :].rearrange("p (c n) -> p c n", n=64)),
               reads=["state_f"], writes=["state_b"])
            op("vector", lambda e: e.tensor_copy(
                out=state_b.ap()[64:128, :].rearrange("p (c n) -> p c n", n=64)[:, 1:16:2, :],
                in_=state_f.ap()[64:128, :].rearrange("p (c n) -> p c n", n=64)),
               reads=["state_f"], writes=["state_b"])
            op("vector", lambda e: e.tensor_copy(out=ksr_b[0:1, :], in_=ksr_f[:]),
               reads=["ksr_f"], writes=["ksr_b"])

        snap_a1 = S.snapshot()
        for engname in ("sync", "tensor", "vector", "scalar", "gpsimd"):
            S.wait_all(engname, snap_a1)

        op("vector", lambda e: e.memset(z_sb[:], 0.0), writes=["z_sb"])
        op("sync", lambda e: e.dma_start(out=z_sb[0:1, 0:D], in_=fc2b_h.ap()), dma=True, sem="sr", writes=["z_sb"])
        for (rowbuf, rowname, dst, dname, scale) in (
            (z_sb, "z_sb", b2h_b, "b2h_b", 0.5),
        ):
            for s in range(2):
                op("tensor", lambda e, rowbuf=rowbuf, s=s: e.matmul(
                    PS1[:, 512 * s:512 * s + 512], ones_ff[:], rowbuf[:, 512 * s:512 * s + 512],
                    start=True, stop=True),
                   reads=["ones_ff", rowname], writes=[bank(1, s)])
            op("scalar", lambda e, dst=dst, scale=scale: e.activation(
                out=dst[:], in_=PS1[:, 0:1024], func=AF.Copy, scale=scale),
               reads=["p1a", "p1b"], writes=[dname])

        for c in range(n_own):
            if c % 4 == 0 and c >= 4 and (c // 4 - 1) < n_b:
                drain_b(len(B_QUEUE))
                if PENDING[0] is not None:
                    PENDING[0]()
                    PENDING[0] = None
                emit_b_super(c // 4 - 1)
            rem = n_own - c
            budget = (len(B_QUEUE) + rem - 1) // rem if B_QUEUE else 0
            b3 = (budget + 2) // 3
            xbuf, xname = (xa, "xa") if c % 2 == 0 else (xb, "xb")
            if c == 0:
                x_load_transpose(xo_h, 0, xbuf, xname, 1)
            for d in range(8):
                for s in range(2):
                    op("tensor", lambda e, d=d, s=s: e.matmul(
                        PS1[:, 512 * s:512 * s + 512],
                        x_fma[:, 128 * d:128 * d + 128],
                        wqkv[:, 3072 * d + 512 * s: 3072 * d + 512 * s + 512],
                        start=(d == 0), stop=(d == 7)),
                       reads=["x_fma", f"wqkv{d}"], writes=[bank(1, s)])
            op("scalar", lambda e: e.activation(out=t1[:], in_=PS1[:, 0:1024], func=AF.Relu, scale=-1.0),
               reads=["p1a", "p1b"], writes=["t1"])
            op("scalar", lambda e: e.activation(out=t2[:], in_=t1[:], func=AF.Exp, scale=-1.0),
               reads=["t1"], writes=["t2"])
            op("scalar", lambda e: e.activation(out=t1[:], in_=PS1[:, 0:1024], func=AF.Relu),
               reads=["p1a", "p1b"], writes=["t1"])
            op("vector", lambda e: e.tensor_tensor(out=qhat[:], in0=t2[:], in1=t1[:], op=ALU.add),
               reads=["t2", "t1"], writes=["qhat"])
            kv_mms(0, x_fma, "x_fma")
            if c + 1 < n_own:
                xb2, xn2 = (xa, "xa") if (c + 1) % 2 == 0 else (xb, "xb")
                x_load_transpose(xo_h, c + 1, xb2, xn2, 1)
            elu_v(0)
            drain_b(b3)
            for d in range(8):
                op("tensor", lambda e, d=d: e.matmul(PS1[:, 1024 + 128 * d:1152 + 128 * d],
                                                     khat[:, 128 * d:128 * d + 128], ident[:],
                                                     start=True, stop=True),
                   reads=["khat", "ident"], writes=banksr(1, 1024 + 128 * d, 1152 + 128 * d))
            op("vector", lambda e: e.tensor_copy(
                out=k_fm.ap()[0:64, :].rearrange("p (cc n) -> p cc n", n=128)[:, 0:16:2, :],
                in_=PS1.ap()[0:64, 1024:2048].rearrange("p (cc n) -> p cc n", n=128)),
               reads=["p1c", "p1d"], writes=["k_fm"])
            op("vector", lambda e: e.tensor_copy(
                out=k_fm.ap()[64:128, :].rearrange("p (cc n) -> p cc n", n=128)[:, 1:16:2, :],
                in_=PS1.ap()[64:128, 1024:2048].rearrange("p (cc n) -> p cc n", n=128)),
               reads=["p1c", "p1d"], writes=["k_fm"])
            for s in range(2):
                bk = bank(1, s)
                op("tensor", lambda e, s=s: e.matmul(PS1[:, 512 * s:512 * s + 512], umask[:],
                                                     khat[:, 512 * s:512 * s + 512],
                                                     start=True, stop=False),
                   reads=["umask", "khat"], writes=[bk])
                op("tensor", lambda e, s=s: e.matmul(PS1[:, 512 * s:512 * s + 512], ones_fb[:],
                                                     ksr_b[:, 512 * s:512 * s + 512],
                                                     start=False, stop=True),
                   reads=["ones_fb", "ksr_b"], writes=[bk])
            drain_b(b3)
            op("vector", lambda e: e.tensor_tensor(out=t2[:], in0=PS1[:, 0:1024], in1=qhat[:], op=ALU.mult),
               reads=["p1a", "p1b", "qhat"], writes=["t2"])
            op("vector", lambda e: e.tensor_reduce(out=dr32[:, 0:16],
                                                   in_=t2.ap().rearrange("p (h d) -> p h d", d=DH),
                                                   axis=AX.X, op=ALU.add),
               reads=["t2"], writes=["dr32"])
            op("vector", lambda e: e.tensor_scalar_add(dr32[:, 16:32], dr32[:, 0:16], EPS_ATTN),
               reads=["dr32"], writes=["dr32"])
            op("vector", lambda e: e.reciprocal(out=dr32[:, 0:16], in_=dr32[:, 16:32]),
               reads=["dr32"], writes=["dr32"])
            op("vector", lambda e: e.tensor_tensor(
                out=qp_tm.ap().rearrange("p (h d) -> p h d", d=DH),
                in0=qhat.ap().rearrange("p (h d) -> p h d", d=DH),
                in1=dr32.ap()[:, 0:16].unsqueeze(2).to_broadcast([128, 16, DH]),
                op=ALU.mult),
               reads=["qhat", "dr32"], writes=["qp_tm"])
            for d in range(8):
                op("tensor", lambda e, d=d: e.matmul(PS1[:, 1024 + 128 * d:1152 + 128 * d],
                                                     qp_tm[:, 128 * d:128 * d + 128], ident[:],
                                                     start=True, stop=True),
                   reads=["qp_tm", "ident"], writes=banksr(1, 1024 + 128 * d, 1152 + 128 * d))
            op("scalar", lambda e: e.copy(out=qp_fm[:, 0:512], in_=PS1[:, 1024:1536]),
               reads=["p1c"], writes=["qp_fm"])
            op("scalar", lambda e: e.copy(out=qp_fm[:, 512:1024], in_=PS1[:, 1536:2048]),
               reads=["p1d"], writes=["qp_fm"])
            for half in range(2):
                for h in range(8 * half, 8 * half + 8):
                    fo = 128 * (h // 2)
                    op("tensor", lambda e, h=h, fo=fo: e.matmul(
                        PS0[:, 128 * (h % 8):128 * (h % 8) + 128],
                        k_fm[:, 128 * h:128 * h + 128], qp_fm[:, fo:fo + 128],
                        start=True, stop=True),
                       reads=["k_fm", "qp_fm"], writes=banksr(0, 128 * (h % 8), 128 * (h % 8) + 128))
                op("vector", lambda e: e.tensor_tensor(
                    out=st_m.ap().rearrange("p (h t) -> p h t", t=128),
                    in0=PS0.ap()[:, 0:1024].rearrange("p (h t) -> p h t", t=128),
                    in1=umask.ap().unsqueeze(1).to_broadcast([128, 8, 128]),
                    op=ALU.mult),
                   reads=["p0a", "p0b", "umask"], writes=["st_m"])
                if half == 0:
                    drain_b(b3)
                for h in range(8 * half, 8 * half + 8):
                    pb, fo = 64 * (h % 2), 128 * (h // 2)
                    bk = "p1a" if h // 2 < 4 else "p1b"
                    op("tensor", lambda e, h=h, pb=pb, fo=fo: e.matmul(
                        PS1[pb:pb + 64, fo:fo + 128],
                        v_tm[:, 64 * h:64 * h + 64], st_m[:, 128 * (h % 8):128 * (h % 8) + 128],
                        start=True, stop=False),
                       reads=["v_tm", "st_m"], writes=[bk])
                    op("tensor", lambda e, h=h, pb=pb, fo=fo: e.matmul(
                        PS1[pb:pb + 64, fo:fo + 128],
                        state_b[:, 64 * h:64 * h + 64],
                        qp_fm[:, fo:fo + 128],
                        start=False, stop=True),
                       reads=["state_b", "qp_fm"], writes=[bk])
            op("scalar", lambda e: e.copy(out=attn[:, 0:512], in_=PS1[:, 0:512]),
               reads=["p1a"], writes=["attn"])
            op("scalar", lambda e: e.copy(out=attn[:, 512:1024], in_=PS1[:, 512:1024]),
               reads=["p1b"], writes=["attn"])
            state_update(0, prefix=False)
            drain_b(b3)
            for d in range(8):
                for s in range(2):
                    op("tensor", lambda e, d=d, s=s: e.matmul(
                        PS1[:, 512 * s:512 * s + 512],
                        attn[:, 128 * d:128 * d + 128],
                        wo_s[:, 1024 * d + 512 * s: 1024 * d + 512 * s + 512],
                        start=(d == 0), stop=(d == 7)),
                       reads=["attn", f"wo{d}"], writes=[bank(1, s)])
            op("vector", lambda e, xbuf=xbuf: e.scalar_tensor_tensor(
                out=r_sb[:], in0=PS1[:, 0:1024], scalar=1.0, in1=xbuf[:],
                op0=ALU.mult, op1=ALU.add, accum_out=stx[:, 0:1]),
               reads=["p1a", "p1b", xname], writes=["r_sb", "stx"])
            op("vector", lambda e: e.scalar_tensor_tensor(
                out=t2[:], in0=r_sb[:], scalar=1.0, in1=r_sb[:],
                op0=ALU.mult, op1=ALU.mult, accum_out=stx[:, 1:2]),
               reads=["r_sb"], writes=["t2", "stx"])
            tq4 = c % 4
            x1p, x1pn = (x1_s, "x1_s") if (c // 4) % 2 == 0 else (x1_s2, "x1_s2")
            layer_norm_tail(r_sb[:], ["r_sb"], x1p[:, D * tq4:D * tq4 + D], x1pn,
                            stx[:, 0:1], stx[:, 1:2], stx[:, 2:3], stx[:, 3:4],
                            "stx", "stx", "stx", "stx",
                            acc_ap=bt1[:, tq4:tq4 + 1], acc_w=["bty"])
            op("vector", lambda e, tq4=tq4, x1p=x1p: e.scalar_tensor_tensor(
                out=u_s[:], in0=x1p[:, D * tq4:D * tq4 + D], scalar=1.0,
                in1=x1p[:, D * tq4:D * tq4 + D], op0=ALU.mult, op1=ALU.mult,
                accum_out=bt2[:, tq4:tq4 + 1]),
               reads=[x1pn], writes=["u_s", "bty"])
            drain_b(b3)
        if n_own == NCH and n_b == 8:
            drain_b(len(B_QUEUE))
            if PENDING[0] is not None:
                PENDING[0]()
                PENDING[0] = None
            emit_b_super(7)
        drain_b(len(B_QUEUE))
        if PENDING[0] is not None:
            PENDING[0]()
            PENDING[0] = None

        S.wait_all("sync", S.snapshot())

        with nc.Block() as blk:
            @blk.sync
            def _(eng):
                S.replay("sync", eng, sems)

            @blk.tensor
            def _(eng):
                S.replay("tensor", eng, sems)

            @blk.vector
            def _(eng):
                S.replay("vector", eng, sems)

            @blk.scalar
            def _(eng):
                S.replay("scalar", eng, sems)

            @blk.gpsimd
            def _(eng):
                S.replay("gpsimd", eng, sems)

    return nc


def kernel(L0_attn_x, L0_attn_Wq, L0_attn_Wk, L0_attn_Wv, L0_attn_Wo,
           L0_ffn_ln_gamma, L0_ffn_ln_beta, L0_ffn_fc1_q, L0_ffn_fc1_scales,
           L0_ffn_fc1_bias, L0_ffn_fc2_q, L0_ffn_fc2_scales, L0_ffn_fc2_bias):
    x = np.ascontiguousarray(np.asarray(L0_attn_x, dtype=np.float32))
    wqkvT = np.ascontiguousarray(
        np.concatenate([np.asarray(L0_attn_Wq).T, np.asarray(L0_attn_Wk).T,
                        np.asarray(L0_attn_Wv).T], axis=1).astype(np.float32))
    woT = np.ascontiguousarray(np.asarray(L0_attn_Wo).T.astype(np.float32))
    fc1q = np.ascontiguousarray(np.asarray(L0_ffn_fc1_q, dtype=np.int32))
    fc2q = np.ascontiguousarray(np.asarray(L0_ffn_fc2_q, dtype=np.int32))
    fc1s = np.ascontiguousarray(np.asarray(L0_ffn_fc1_scales, dtype=np.float32))
    fc2s = np.ascontiguousarray(np.asarray(L0_ffn_fc2_scales, dtype=np.float32))
    fc1sb = np.ascontiguousarray(np.broadcast_to(fc1s[:, None, :], (8, 128, F)))
    fc2sb = np.ascontiguousarray(np.broadcast_to(fc2s[:, None, :], (32, 128, D)))
    fc1b = np.ascontiguousarray(np.asarray(L0_ffn_fc1_bias, dtype=np.float32).reshape(32, 128).T)
    fc2b = np.ascontiguousarray(np.asarray(L0_ffn_fc2_bias, dtype=np.float32).reshape(1, D))
    gam = np.ascontiguousarray(np.asarray(L0_ffn_ln_gamma, dtype=np.float32).reshape(1, D))
    bet = np.ascontiguousarray(np.asarray(L0_ffn_ln_beta, dtype=np.float32).reshape(1, D))
    gamc = np.ascontiguousarray(np.asarray(L0_ffn_ln_gamma, dtype=np.float32).reshape(8, 128).T)
    betc = np.ascontiguousarray(np.asarray(L0_ffn_ln_beta, dtype=np.float32).reshape(8, 128).T)
    umaskI = np.triu(np.ones((128, 128), np.float32))
    identI = np.eye(128, dtype=np.float32)
    zeros_x = np.zeros((TOK, D), np.float32)

    nc = build_program()
    in_maps = []
    for c in range(8):
        b, j = c // 2, c % 2
        in_maps.append({
            "xo": np.ascontiguousarray(x[b, TOK * j:TOK * j + TOK]),
            "xp": np.ascontiguousarray(x[b, 0:TOK]) if j == 1 else zeros_x,
            "amask": np.array([[1.0 if j == 1 else 0.0]], np.float32),
            "wqkvT": wqkvT, "woT": woT,
            "fc1q": fc1q, "fc1s": fc1s, "fc1b": fc1b, "fc1sb": fc1sb, "fc2sb": fc2sb,
            "fc2q": fc2q, "fc2s": fc2s, "fc2b": fc2b,
            "gam": gam, "bet": bet, "gamC": gamc, "betC": betc,
            "umaskI": umaskI, "identI": identI,
        })
    res = run_bass_kernel_spmd(nc, in_maps, core_ids=list(range(8)))
    out = np.empty((4, 2 * TOK, D), np.float32)
    for c in range(8):
        b, j = c // 2, c % 2
        out[b, TOK * j:TOK * j + TOK] = np.asarray(res.results[c]["out"])
    if DEBUG:
        global DBG_RES
        DBG_RES = res.results
    return out
